# version 01
# baseline (speedup 1.0000x reference)
import numpy as np
import ml_dtypes
import concourse.bass as bass
import concourse.mybir as mybir
from concourse.bass_utils import run_bass_kernel_spmd

F32 = mybir.dt.float32
BF16 = mybir.dt.bfloat16
AF = mybir.ActivationFunctionType
ALU = mybir.AluOpType
AX = mybir.AxisListType
BF = ml_dtypes.bfloat16

D = 1024
T = 4096
TC = 512
RT = D // 128
CT = T // TC
NCORE = 8
NEG = -1e9

REG = {}
def reg(name, shape, dt):
    REG[name] = (tuple(shape), dt)

reg("x0T", (D, T), np.float32)
for l in range(4):
    reg(f"xres{l+1}", (D, T), np.float32)
    reg(f"xn{l}", (D, T), BF)
    reg(f"s{l}", (D, T), np.float32)
    reg(f"g{l}", (D, T), np.float32)
reg("q1", (D, T), BF); reg("k1", (D, T), BF); reg("vT1", (T, D), BF)
reg("qn1", (D, T), BF); reg("kn1", (D, T), BF)
reg("of1", (D, T), BF)
reg("q3", (D, T), BF); reg("k3", (D, T), BF); reg("vT3", (T, D), BF)
reg("m3", (16, T), np.float32)
reg("of3", (D, T), BF)
reg("t3", (D, T), BF)
reg("comb3", (32, T), np.float32)
reg("u2", (512, T), np.float32)
reg("comb2", (8, T), np.float32)
reg("q0", (D, T), BF); reg("k0", (D, T), BF); reg("v0", (D, T), BF)
reg("qr0", (D, T), BF); reg("kr0", (D, T), BF); reg("vs0", (D, T), BF)
reg("gtok0", (T, D), BF)
reg("gk0", (16, T), np.float32); reg("beta0", (16, T), np.float32)
reg("cgd0", (16, T), np.float32); reg("cgl0", (16, T), np.float32)
reg("gedd0", (16, 32), np.float32)
reg("otok0", (T, D), BF)
reg("ogtok0", (T, D), BF)
reg("xqtok0", (T, D), BF)
reg("h0", (2816, T), np.float32)
reg("hqtok0", (T, 2816), BF)

class Sem:
    def __init__(self, nc, name):
        self.h = nc.alloc_semaphore(name)
        self.v = 0
    def inc(self, instr, n=1):
        instr.then_inc(self.h, n)
        self.v += n

class Ctx:
    def __init__(self, nc, avail, needed_later, host_shapes):
        self.nc = nc
        self.avail = avail
        self.needed_later = needed_later
        self.host_shapes = host_shapes
        self.dram = {}
        self.ext_in = []
        self.ext_out = []
        self.guards = []
        self.uid = 0
        for s in ["LD", "LW", "PE", "ACT", "DVE", "ST"]:
            setattr(self, s, Sem(nc, s))
        self.WB = [Sem(nc, f"WB{i}") for i in range(3)]

    def u(self, base):
        self.uid += 1
        return f"{base}_{self.uid}"

    def din(self, name):
        if name in self.dram:
            return self.dram[name]
        assert name in self.avail, f"{name} not available to this launch"
        shape, dt = REG[name]
        mdt = BF16 if dt == BF else F32
        ap = self.nc.dram_tensor(name, list(shape), mdt, kind="ExternalInput").ap()
        self.dram[name] = ap
        self.ext_in.append(name)
        return ap

    def dout(self, name):
        if name in self.dram:
            return self.dram[name]
        shape, dt = REG[name]
        mdt = BF16 if dt == BF else F32
        kind = "ExternalOutput" if name in self.needed_later else "Internal"
        ap = self.nc.dram_tensor(name, list(shape), mdt, kind=kind).ap()
        self.dram[name] = ap
        if kind == "ExternalOutput":
            self.ext_out.append(name)
        return ap

    def win(self, name):
        if name in self.dram:
            return self.dram[name]
        ap = self.nc.dram_tensor(name, list(self.host_shapes[name]), F32,
                                 kind="ExternalInput").ap()
        self.dram[name] = ap
        self.ext_in.append(name)
        return ap

    def sbuf(self, name, shape, dt, persist=False):
        g = self.nc.sbuf_tensor(self.u(name), list(shape), dt,
                                side="right" if persist else "left")
        h = g.__enter__()
        ap = h.ap()
        self.guards.append([ap, g, persist])
        return ap

    def psum(self, name, shape, dt=F32):
        g = self.nc.psum_tensor(self.u(name), list(shape), dt)
        h = g.__enter__()
        ap = h.ap()
        self.guards.append([ap, g, False])
        return ap

    def free(self, *aps):
        todo = [e for e in reversed(self.guards) if any(e[0] is ap for ap in aps)]
        for e in todo:
            e[1].__exit__(None, None, None)
            self.guards.remove(e)

    def cbias(self, val, p=128):
        if not hasattr(self, "_cb"):
            self._cb = {}
            self._cbuf = self.sbuf("cbconst", [128, 16], F32, persist=True)
            self._cbn = 0
        key = float(val)
        if key not in self._cb:
            col = self._cbn
            self._cbn += 1
            assert col < 16
            i = self.nc.vector.memset(self._cbuf[:, col:col + 1], key)
            self.DVE.inc(i, 1)
            self.nc.scalar.wait_ge(self.DVE.h, self.DVE.v)
            self._cb[key] = col
        col = self._cb[key]
        return self._cbuf[0:p, col:col + 1]

    def quiesce(self):
        nc = self.nc
        if self.ST.v:
            nc.sync.wait_ge(self.ST.h, self.ST.v)
        if self.LD.v:
            nc.sync.wait_ge(self.LD.h, self.LD.v)
        if self.LW.v:
            nc.gpsimd.wait_ge(self.LW.h, self.LW.v)

    def end_stage(self):
        self.quiesce()
        self.nc.all_engine_barrier()
        rest = [e for e in reversed(self.guards) if not e[2]]
        for e in rest:
            e[1].__exit__(None, None, None)
            self.guards.remove(e)


def rview(ap, p=128):
    return ap.rearrange("(r p) t -> p r t", p=p)

def load_packed(C, dram_ap, rows, cols, dt=BF16, name="pk", guard=True):
    nc = C.nc
    sb = C.sbuf(name, [128, (rows // 128) * cols], dt)
    dst = sb.rearrange("p (r t) -> p r t", t=cols)
    if guard:
        if C.PE.v:
            nc.gpsimd.wait_ge(C.PE.h, C.PE.v)
        if C.DVE.v:
            nc.gpsimd.wait_ge(C.DVE.h, C.DVE.v)
    C.LW.inc(nc.gpsimd.dma_start(dst, rview(dram_ap)), 16)
    return sb

def load_cols(C, vec_ap, n, dt=F32, name="col"):
    nc = C.nc
    p = min(n, 128)
    r = (n + 127) // 128
    sb = C.sbuf(name, [p, r], dt)
    src = vec_ap.rearrange("(r p) -> p r", p=p)
    if C.PE.v:
        nc.gpsimd.wait_ge(C.PE.h, C.PE.v)
    if C.DVE.v:
        nc.gpsimd.wait_ge(C.DVE.h, C.DVE.v)
    C.LW.inc(nc.gpsimd.dma_start(sb[:, :], src, allow_slow_non_contiguous=True), 16)
    return sb


def gemm_T(C, *, terms, din, dout, out_cb, nbanks=8, tag="g", nT=T, cw=TC,
           wait_first=False, chunks=None):
    nc = C.nc
    KT = din // 128
    MT = (dout + 127) // 128
    if chunks is None:
        chunks = list(range(nT // cw))
    CTn = len(chunks)
    NW = 3
    mwmax = min(128, dout)
    wbufs = {}
    for i, (kind, w, x) in enumerate(terms):
        if kind == "dram":
            wbufs[i] = [C.sbuf(f"{tag}w{i}", [128, KT * mwmax], BF16) for _ in range(NW)]
    ps = [C.psum(f"{tag}ps", [128, cw]) for _ in range(nbanks)]

    wload_at = {}

    def load_w_mt(mtile):
        buf = mtile % NW
        for i in wbufs:
            w = terms[i][1]
            m0 = mtile * 128
            m1 = min(m0 + 128, dout)
            srcw = w[:, m0:m1].rearrange("(k p) m -> p k m", p=128)
            dst = wbufs[i][buf][:, 0:KT * (m1 - m0)].rearrange("p (k m) -> p k m", m=(m1 - m0))
            C.WB[buf].inc(nc.gpsimd.dma_start(dst, srcw), 16)
        wload_at[mtile] = C.WB[buf].v

    pe0 = C.PE.v
    lw0 = C.LW.v
    pe_after = {}
    nstr = len(wbufs)
    if nstr and C.PE.v:
        nc.gpsimd.wait_ge(C.PE.h, C.PE.v)
    for mt in range(min(NW, MT)):
        if nstr:
            load_w_mt(mt)
    if wait_first:
        nc.tensor.wait_ge(C.DVE.h, C.DVE.v)
        if C.LW.v:
            nc.tensor.wait_ge(C.LW.h, lw0)

    bank_free = [None] * nbanks
    ib = 0
    for mt in range(MT):
        m0 = mt * 128
        mw = min(128, dout - m0)
        if nstr:
            nc.tensor.wait_ge(C.WB[mt % NW].h, wload_at[mt])
        for c in chunks:
            bank = ib % nbanks
            if bank_free[bank] is not None:
                nc.tensor.wait_ge(C.DVE.h, bank_free[bank])
            first = True
            for ti, (kind, w, x) in enumerate(terms):
                for k in range(KT):
                    if kind == "dram":
                        lhsT = wbufs[ti][mt % NW][:, k * mw:(k + 1) * mw]
                    else:
                        lhsT = w[:, k * dout + m0: k * dout + m0 + mw]
                    mi = nc.tensor.matmul(ps[bank][0:mw, :],
                                          lhsT,
                                          x[:, k * nT + c * cw: k * nT + (c + 1) * cw],
                                          start=first,
                                          stop=(ti == len(terms) - 1 and k == KT - 1))
                    first = False
            C.PE.inc(mi, 1)
            last = out_cb(mt, c, ps[bank][0:mw, :])
            C.DVE.inc(last, 1)
            bank_free[bank] = C.DVE.v
            ib += 1
        pe_after[mt] = C.PE.v
        nmt = mt + NW
        if nmt < MT and nstr:
            nc.gpsimd.wait_ge(C.PE.h, pe_after[nmt - NW])
            load_w_mt(nmt)
    C.free(*ps, *[b for i in wbufs for b in wbufs[i]])

def st_proj_T(C, *, w, src_sb, out_name=None, out_dram=None, evac="copy",
              bias_cols=None, scale=None, scale_cols=None, tag="p", din=None, dout=None):
    nc = C.nc
    if isinstance(w, str):
        wd = C.win(w)
        din, dout = C.host_shapes[w]
        term = ("dram", wd, src_sb)
    else:
        term = (w[0], w[1], src_sb)
    out = C.dout(out_name) if out_name else out_dram
    odt = REG[out_name][1] if out_name else np.float32
    mdt = BF16 if odt == BF else F32
    obuf = [C.sbuf(f"{tag}ob", [128, T], mdt) for _ in range(2)]
    stg = {}

    def cb(mt, c, ps):
        mw = ps.shape[0]
        dst = obuf[mt % 2][0:mw, c * TC:(c + 1) * TC]
        eng = nc.vector if evac == "scale_cols" else nc.scalar
        if c == 0 and mt >= 2:
            eng.wait_ge(C.ST.h, C.ST.v)
        eng.wait_ge(C.PE.h, C.PE.v)
        if evac == "copy":
            i = nc.scalar.copy(dst, ps)
        elif evac == "silu_bias":
            i = nc.scalar.activation(dst, ps, AF.Silu, bias=bias_cols[:, mt:mt + 1])
        elif evac == "scale":
            i = nc.scalar.activation(dst, ps, AF.Copy, scale=scale)
        elif evac == "scale_cols":
            i = nc.vector.tensor_scalar_mul(dst, ps, scale_cols[:, mt:mt + 1])
        if c == CT - 1:
            nc.sync.wait_ge(C.DVE.h, C.DVE.v + 1)
            C.ST.inc(nc.sync.dma_start(out[mt * 128:mt * 128 + mw, :], obuf[mt % 2][0:mw, :]), 16)
            stg[mt] = C.ST.v
        return i

    gemm_T(C, terms=[term], din=din, dout=dout, out_cb=cb, tag=tag, wait_first=True)


def st_layernorm(C, l):
    nc = C.nc
    src = C.din("x0T" if l == 0 else f"xres{l}")
    dst = C.dout(f"xn{l}")
    xs = [C.sbuf("ln_x", [128, RT * TC], F32) for _ in range(2)]
    sq = C.sbuf("ln_sq", [128, RT * TC], F32)
    ones = C.sbuf("ln_ones", [128, 1], F32)
    onesr = C.sbuf("ln_onesr", [1, 128], F32)
    ps_s = C.psum("ln_ps", [1, TC])
    ps_q = C.psum("ln_pq", [1, TC])
    mu = C.sbuf("ln_mu", [1, TC], F32)
    qq = C.sbuf("ln_qq", [1, TC], F32)
    va = C.sbuf("ln_va", [1, TC], F32)
    rs = C.sbuf("ln_rs", [1, TC], F32)
    mub = C.psum("ln_mub", [128, TC])
    rb = C.psum("ln_rb", [128, TC])
    xn = C.sbuf("ln_xn", [128, RT * TC], BF16)
    tmp = C.sbuf("ln_tmp", [128, TC], F32)
    nc.vector.memset(ones[:, :], 1.0)
    C.DVE.inc(nc.vector.memset(onesr[:, :], 1.0), 1)
    nc.tensor.wait_ge(C.DVE.h, C.DVE.v)

    def load(c):
        C.LD.inc(nc.sync.dma_start(xs[c % 2].rearrange("p (r t) -> p r t", t=TC),
                                   src[:, c * TC:(c + 1) * TC].rearrange("(r p) t -> p r t", p=128)), 16)

    load(0)
    ld_done = {0: C.LD.v}
    for c in range(CT):
        x = xs[c % 2]
        nc.scalar.wait_ge(C.LD.h, C.LD.v)
        nc.scalar.wait_ge(C.PE.h, C.PE.v)
        for r in range(RT):
            si = nc.scalar.activation(sq[:, r * TC:(r + 1) * TC], x[:, r * TC:(r + 1) * TC], AF.Square)
        C.ACT.inc(si, 1)
        nc.tensor.wait_ge(C.ACT.h, C.ACT.v)
        nc.tensor.wait_ge(C.DVE.h, C.DVE.v)
        for r in range(RT):
            nc.tensor.matmul(ps_s[:, :], ones[:, :], x[:, r * TC:(r + 1) * TC],
                             start=(r == 0), stop=(r == RT - 1))
        for r in range(RT):
            mi = nc.tensor.matmul(ps_q[:, :], ones[:, :], sq[:, r * TC:(r + 1) * TC],
                                  start=(r == 0), stop=(r == RT - 1))
        C.PE.inc(mi, 1)
        nc.vector.wait_ge(C.PE.h, C.PE.v)
        nc.vector.tensor_scalar(mu[:, :], ps_s[:, :], 1.0 / D, None, ALU.mult)
        nc.vector.tensor_scalar(qq[:, :], ps_q[:, :], 1.0 / D, None, ALU.mult)
        nc.vector.tensor_mul(va[:, :], mu[:, :], mu[:, :])
        C.DVE.inc(nc.vector.tensor_sub(va[:, :], qq[:, :], va[:, :]), 1)
        nc.scalar.wait_ge(C.DVE.h, C.DVE.v)
        C.ACT.inc(nc.scalar.activation(rs[:, :], va[:, :], AF.Sqrt, bias=C.cbias(1e-5, p=1)), 1)
        nc.vector.wait_ge(C.ACT.h, C.ACT.v)
        C.DVE.inc(nc.vector.reciprocal(rs[:, :], rs[:, :]), 1)
        nc.tensor.wait_ge(C.DVE.h, C.DVE.v)
        nc.tensor.matmul(mub[:, :], onesr[:, :], mu[:, :], start=True, stop=True)
        mi = nc.tensor.matmul(rb[:, :], onesr[:, :], rs[:, :], start=True, stop=True)
        C.PE.inc(mi, 1)
        if c + 1 < CT:
            load(c + 1)
            ld_done[c + 1] = C.LD.v
        nc.vector.wait_ge(C.PE.h, C.PE.v)
        for r in range(RT):
            nc.vector.tensor_sub(tmp[:, :], x[:, r * TC:(r + 1) * TC], mub[:, :])
            mi2 = nc.vector.tensor_mul(xn[:, r * TC:(r + 1) * TC], tmp[:, :], rb[:, :])
        C.DVE.inc(mi2, 1)
        nc.sync.wait_ge(C.DVE.h, C.DVE.v)
        C.ST.inc(nc.sync.dma_start(dst[:, c * TC:(c + 1) * TC].rearrange("(r p) t -> p r t", p=128),
                                   xn.rearrange("p (r t) -> p r t", t=TC)), 16)
        nc.vector.wait_ge(C.ST.h, C.ST.v)

def st_update(C, l):
    nc = C.nc
    xi = C.din("x0T" if l == 0 else f"xres{l}")
    s = C.din(f"s{l}")
    g = C.din(f"g{l}")
    xo = C.dout(f"xres{l + 1}")
    bx = [C.sbuf("ux", [128, RT * TC], F32) for _ in range(2)]
    bs = [C.sbuf("us", [128, RT * TC], F32) for _ in range(2)]
    bg = [C.sbuf("ug", [128, RT * TC], F32) for _ in range(2)]
    ob = [C.sbuf("uo", [128, RT * TC], F32) for _ in range(2)]
    dve_end = {}

    def load(c):
        p = c % 2
        if c >= 2:
            nc.sync.wait_ge(C.DVE.h, dve_end[c - 2])
            nc.gpsimd.wait_ge(C.DVE.h, dve_end[c - 2])
        for (b, srcd, eng, sem) in ((bx[p], xi, nc.sync, C.LD), (bs[p], s, nc.sync, C.LD),
                                    (bg[p], g, nc.gpsimd, C.LW)):
            sem.inc(eng.dma_start(b.rearrange("p (r t) -> p r t", t=TC),
                                  srcd[:, c * TC:(c + 1) * TC].rearrange("(r p) t -> p r t", p=128)), 16)

    load(0)
    ld_at = {0: (C.LD.v, C.LW.v)}
    for c in range(CT):
        p = c % 2
        if c + 1 < CT:
            load(c + 1)
            ld_at[c + 1] = (C.LD.v, C.LW.v)
        nc.vector.wait_ge(C.LD.h, C.LD.v)
        nc.vector.wait_ge(C.LW.h, C.LW.v)
        if c >= 2:
            nc.vector.wait_ge(C.ST.h, C.ST.v)
        for r in range(RT):
            sl = slice(r * TC, (r + 1) * TC)
            nc.vector.tensor_mul(ob[p][:, sl], bs[p][:, sl], bg[p][:, sl])
            mi = nc.vector.tensor_add(ob[p][:, sl], ob[p][:, sl], bx[p][:, sl])
        C.DVE.inc(mi, 1)
        dve_end[c] = C.DVE.v
        nc.sync.wait_ge(C.DVE.h, C.DVE.v)
        C.ST.inc(nc.sync.dma_start(xo[:, c * TC:(c + 1) * TC].rearrange("(r p) t -> p r t", p=128),
                                   ob[p].rearrange("p (r t) -> p r t", t=TC)), 16)


def proj_tok(C, w_name, xn_sb, out_name, tag="vt"):
    nc = C.nc
    wv = load_packed(C, C.win(w_name), D, D, name=tag + "w")
    out = C.dout(out_name)
    ps = [C.psum(tag + "ps", [128, TC]) for _ in range(4)]
    ob = [C.sbuf(tag + "ob", [128, D], BF16) for _ in range(2)]
    nc.tensor.wait_ge(C.LW.h, C.LW.v)
    bank_free = [None] * 4
    stg = {}
    ib = 0
    for tt in range(T // 128):
        for n in range(D // TC):
            bank = ib % 4
            if bank_free[bank] is not None:
                nc.tensor.wait_ge(C.DVE.h, bank_free[bank])
            for k in range(RT):
                mi = nc.tensor.matmul(ps[bank][:, :],
                                      xn_sb[:, k * T + tt * 128: k * T + tt * 128 + 128],
                                      wv[:, k * D + n * TC: k * D + (n + 1) * TC],
                                      start=(k == 0), stop=(k == RT - 1))
            C.PE.inc(mi, 1)
            if n == 0 and tt >= 2:
                nc.scalar.wait_ge(C.ST.h, C.ST.v)
            nc.scalar.wait_ge(C.PE.h, C.PE.v)
            i = nc.scalar.copy(ob[tt % 2][:, n * TC:(n + 1) * TC], ps[bank][:, :])
            C.DVE.inc(i, 1)
            bank_free[bank] = C.DVE.v
            ib += 1
        nc.sync.wait_ge(C.DVE.h, C.DVE.v)
        C.ST.inc(nc.sync.dma_start(out[tt * 128:(tt + 1) * 128, :], ob[tt % 2][:, :]), 16)
        stg[tt] = C.ST.v
    C.free(*ps, *ob, wv)

def st_proj1(C):
    nc = C.nc
    xn_sb = load_packed(C, C.din("xn1"), D, T, name="xnsb")
    st_proj_T(C, w="L1_attn_Wq", src_sb=xn_sb, out_name="q1", tag="pq")
    st_proj_T(C, w="L1_attn_Wk", src_sb=xn_sb, out_name="k1", tag="pk")
    proj_tok(C, "L1_attn_Wv", xn_sb, "vT1", tag="vt1")

def st_l2norm1(C):
    nc = C.nc
    hm = C.sbuf("hmask", [128, 2], F32)
    bc = C.sbuf("bmask", [2, 128], F32)
    C.LW.inc(nc.gpsimd.dma_start(hm[:, :], C.win("c_hmask2")[:, :]), 16)
    C.LW.inc(nc.gpsimd.dma_start(bc[:, :], C.win("c_bc2")[:, :]), 16)
    for (src_n, dst_n, scale) in (("q1", "qn1", 8.0), ("k1", "kn1", 1.0)):
        src = C.din(src_n)
        dst = C.dout(dst_n)
        xin = load_packed(C, src, D, T, name="l2x")
        xo = C.sbuf("l2o", [128, RT * TC], BF16)
        sq = C.sbuf("l2sq", [128, TC], F32)
        pn = C.psum("l2pn", [2, TC])
        nr = C.sbuf("l2nr", [2, TC], F32)
        rb = C.psum("l2rb", [128, TC])
        nc.scalar.wait_ge(C.LW.h, C.LW.v)
        nc.tensor.wait_ge(C.LW.h, C.LW.v)
        for c in range(CT):
            for r in range(RT):
                xt = xin[:, r * T + c * TC: r * T + (c + 1) * TC]
                nc.scalar.wait_ge(C.PE.h, C.PE.v)
                C.ACT.inc(nc.scalar.activation(sq[:, :], xt, AF.Square), 1)
                nc.tensor.wait_ge(C.ACT.h, C.ACT.v)
                C.PE.inc(nc.tensor.matmul(pn[:, :], hm[:, :], sq[:, :], start=True, stop=True), 1)
                nc.scalar.wait_ge(C.PE.h, C.PE.v)
                C.ACT.inc(nc.scalar.activation(nr[:, :], pn[:, :], AF.Sqrt), 1)
                nc.vector.wait_ge(C.ACT.h, C.ACT.v)
                nc.vector.tensor_scalar_max(nr[:, :], nr[:, :], 1e-12)
                i = nc.vector.reciprocal(nr[:, :], nr[:, :])
                if scale != 1.0:
                    i = nc.vector.tensor_scalar_mul(nr[:, :], nr[:, :], scale)
                C.DVE.inc(i, 1)
                nc.tensor.wait_ge(C.DVE.h, C.DVE.v)
                C.PE.inc(nc.tensor.matmul(rb[:, :], bc[:, :], nr[:, :], start=True, stop=True), 1)
                nc.vector.wait_ge(C.PE.h, C.PE.v)
                C.DVE.inc(nc.vector.tensor_mul(xo[:, r * TC:(r + 1) * TC], xt, rb[:, :]), 1)
            nc.sync.wait_ge(C.DVE.h, C.DVE.v)
            C.ST.inc(nc.sync.dma_start(dst[:, c * TC:(c + 1) * TC].rearrange("(r p) t -> p r t", p=128),
                                       xo.rearrange("p (r t) -> p r t", t=TC)), 16)
            nc.vector.wait_ge(C.ST.h, C.ST.v)
        C.free(xin, xo, sq, pn, nr, rb)

def st_attn1(C):
    nc = C.nc
    qd, kd, vd = C.din("qn1"), C.din("kn1"), C.din("vT1")
    out = C.dout("of1")
    mmc = C.sbuf("mmc", [128, 1536], BF16)
    C.LW.inc(nc.gpsimd.dma_start(mmc[:, :], C.win("c_mm")[:, :]), 16)
    kh = [C.sbuf("kh", [64, T], BF16) for _ in range(2)]
    qh = [C.sbuf("qh", [64, T], BF16) for _ in range(2)]
    va = [C.sbuf("va", [128, 32 * 65], BF16) for _ in range(2)]
    eb = [C.sbuf("eb", [128, TC], BF16) for _ in range(2)]
    psS = [C.psum("psS", [128, TC]) for _ in range(2)]
    psO = C.psum("psO", [65, TC])
    psR = C.psum("psR", [64, TC])
    rec = C.sbuf("rec", [1, TC], F32)
    osb = C.sbuf("osb", [64, TC], F32)
    ob = [C.sbuf("obh", [64, T], BF16) for _ in range(2)]
    onesr = C.sbuf("onesr1", [1, 64], F32)
    nc.vector.memset(onesr[:, :], 1.0)

    def load_head(h):
        p = h % 2
        C.LD.inc(nc.sync.dma_start(kh[p][:, :], kd[h * 64:(h + 1) * 64, :]), 16)
        C.LD.inc(nc.sync.dma_start(qh[p][:, :], qd[h * 64:(h + 1) * 64, :]), 16)
        C.LD.inc(nc.sync.dma_start(va[p].rearrange("p (n e) -> p n e", e=65)[:, :, 0:64],
                                   vd[:, h * 64:(h + 1) * 64].rearrange("(n p) d -> p n d", p=128)), 16)

    load_head(0)
    ld_at = {0: C.LD.v}
    pe_done = {}
    st_done = {}
    for h in range(16):
        p = h % 2
        if h + 1 < 16:
            if h >= 1:
                nc.sync.wait_ge(C.PE.h, pe_done[h - 1])
            load_head(h + 1)
            ld_at[h + 1] = C.LD.v
        nc.vector.wait_ge(C.LD.h, C.LD.v)
        if h >= 2:
            nc.vector.wait_ge(C.ST.h, C.ST.v)
        C.DVE.inc(nc.vector.memset(va[p].rearrange("p (n e) -> p n e", e=65)[:, :, 64:65], 1.0), 1)
        nc.tensor.wait_ge(C.DVE.h, C.DVE.v)
        for c in range(CT):
            ntk = 4 * (c + 1)
            exp_done = [None, None]
            for j in range(ntk):
                sb = j % 2
                if exp_done[sb] is not None:
                    nc.tensor.wait_ge(C.ACT.h, exp_done[sb])
                mi = nc.tensor.matmul(psS[sb][:, :], kh[p][:, j * 128:(j + 1) * 128],
                                      qh[p][:, c * TC:(c + 1) * TC], start=True, stop=True)
                C.PE.inc(mi, 1)
                nc.scalar.wait_ge(C.PE.h, C.PE.v)
                C.ACT.inc(nc.scalar.activation(eb[sb][:, :], psS[sb][:, :], AF.Exp), 1)
                exp_done[sb] = C.ACT.v
                diag = j >= ntk - 4
                if diag:
                    s3 = j * 128 - c * TC
                    nc.vector.wait_ge(C.ACT.h, C.ACT.v)
                    C.DVE.inc(nc.vector.tensor_mul(eb[sb][:, :], eb[sb][:, :],
                                                   mmc[:, 512 - s3:1024 - s3]), 1)
                    nc.tensor.wait_ge(C.DVE.h, C.DVE.v)
                else:
                    nc.tensor.wait_ge(C.ACT.h, C.ACT.v)
                mi = nc.tensor.matmul(psO[:, :], va[p][:, j * 65:(j + 1) * 65], eb[sb][:, :],
                                      start=(j == 0), stop=(j == ntk - 1))
                C.PE.inc(mi, 1)
            nc.vector.wait_ge(C.PE.h, C.PE.v)
            i = nc.vector.reciprocal(rec[:, :], psO[64:65, :])
            C.DVE.inc(nc.vector.tensor_copy(osb[:, :], psO[0:64, :]), 1)
            nc.tensor.wait_ge(C.DVE.h, C.DVE.v)
            C.PE.inc(nc.tensor.matmul(psR[:, :], onesr[:, :], rec[:, :], start=True, stop=True), 1)
            nc.vector.wait_ge(C.PE.h, C.PE.v)
            C.DVE.inc(nc.vector.tensor_mul(ob[p][:, c * TC:(c + 1) * TC], osb[:, :], psR[:, :]), 1)
            nc.tensor.wait_ge(C.DVE.h, C.DVE.v)
        pe_done[h] = C.PE.v
        nc.sync.wait_ge(C.DVE.h, C.DVE.v)
        C.ST.inc(nc.sync.dma_start(out[h * 64:(h + 1) * 64, :], ob[p][:, :]), 16)
        st_done[h] = C.ST.v

def st_wo1(C):
    of = load_packed(C, C.din("of1"), D, T, name="ofsb")
    st_proj_T(C, w="L1_attn_Wo", src_sb=of, out_name="s1", tag="wo")

def st_ffn1(C):
    nc = C.nc
    fr = load_packed(C, C.win("c_fr"), D, D, name="fr")
    fi = load_packed(C, C.win("c_fi"), D, D, name="fi")
    fin = load_packed(C, C.win("c_fin"), D, D, name="fin")
    pt = C.sbuf("pt", [128, RT * D], BF16)
    qt = C.sbuf("qt", [128, RT * D], BF16)
    msb = C.sbuf("msb", [128, RT * D], BF16)
    wr = C.win("L1_ffn_w_freq_real")
    wi = C.win("L1_ffn_w_freq_imag")

    def mk_cb(dstb):
        def cb(mt, c, ps):
            nc.scalar.wait_ge(C.PE.h, C.PE.v)
            return nc.scalar.copy(dstb[:, mt * D + c * TC: mt * D + (c + 1) * TC], ps)
        return cb

    gemm_T(C, terms=[("dram", wr, fr), ("dram", wi, fin)], din=D, dout=D,
           out_cb=mk_cb(pt), tag="pt", nT=D, nbanks=4, wait_first=True)
    gemm_T(C, terms=[("dram", wi, fr), ("dram", wr, fi)], din=D, dout=D,
           out_cb=mk_cb(qt), tag="qt", nT=D, nbanks=4, wait_first=True)

    def cbm(mt, c, ps):
        nc.scalar.wait_ge(C.PE.h, C.PE.v)
        return nc.scalar.activation(msb[:, mt * D + c * TC: mt * D + (c + 1) * TC], ps,
                                    AF.Copy, scale=1.0 / D)
    gemm_T(C, terms=[("sb", pt, fr), ("sb", qt, fi)], din=D, dout=D,
           out_cb=cbm, tag="m", nT=D, nbanks=4, wait_first=True)

    xn_sb = load_packed(C, C.din("xn1"), D, T, name="xnf1")
    bias_cols = load_cols(C, C.win("L1_ffn_bias"), D, name="b1c")
    st_proj_T(C, w=("sb", msb), src_sb=xn_sb, out_name="g1", evac="silu_bias",
              bias_cols=bias_cols, tag="y1", din=D, dout=D)


def st_proj3(C):
    nc = C.nc
    xn_sb = load_packed(C, C.din("xn3"), D, T, name="xnsb3")
    lsc = load_cols(C, C.win("L3_attn_logit_scale"), 16, name="lsc")
    e2 = C.sbuf("e2", [16, 8], F32)
    m16 = C.sbuf("m16", [16, 128], F32)
    C.LW.inc(nc.gpsimd.dma_start(e2[:, :], C.win("c_e2")[:, :]), 16)
    C.LW.inc(nc.gpsimd.dma_start(m16[:, :], C.win("c_m16")[:, :]), 16)
    scol = C.sbuf("scol", [16, 1], F32)
    s2 = C.sbuf("s2c", [16, 8], F32)
    psv = C.psum("psv", [128, 8])
    svec = C.sbuf("svec", [128, 8], F32)
    nc.scalar.wait_ge(C.LW.h, C.LW.v)
    C.ACT.inc(nc.scalar.activation(scol[:, :], lsc[:, :], AF.Exp,
                                   bias=C.cbias(-float(np.log(8.0)), p=16)), 1)
    nc.vector.wait_ge(C.ACT.h, C.ACT.v)
    C.DVE.inc(nc.vector.tensor_scalar_mul(s2[:, :], e2[:, :], scol[:, :]), 1)
    nc.tensor.wait_ge(C.DVE.h, C.DVE.v)
    C.PE.inc(nc.tensor.matmul(psv[:, :], m16[:, :], s2[:, :], start=True, stop=True), 1)
    nc.vector.wait_ge(C.PE.h, C.PE.v)
    C.DVE.inc(nc.vector.tensor_copy(svec[:, :], psv[:, :]), 1)
    C.free(psv)
    st_proj_T(C, w="L3_attn_Wq", src_sb=xn_sb, out_name="q3", evac="scale_cols",
              scale_cols=svec, tag="pq3")
    st_proj_T(C, w="L3_attn_Wk", src_sb=xn_sb, out_name="k3", tag="pk3")
    proj_tok(C, "L3_attn_Wv", xn_sb, "vT3", tag="vt3")

def st_attn3(C):
    nc = C.nc
    qd, kd, vd = C.din("q3"), C.din("k3"), C.din("vT3")
    md = C.dout("m3")
    out = C.dout("of3")
    am = C.sbuf("am", [128, 1536], F32)
    am2 = C.sbuf("am2", [128, 1536], F32)
    ids = C.sbuf("ids3", [128, 128], F32)
    C.LW.inc(nc.gpsimd.dma_start(am[:, :], C.win("c_am")[:, :]), 16)
    C.LW.inc(nc.gpsimd.dma_start(am2[:, :], C.win("c_am2")[:, :]), 16)
    C.LW.inc(nc.gpsimd.dma_start(ids[:, :], C.win("c_ident")[:, :]), 16)
    nkT = C.sbuf("nkT", [64, 16], BF16)
    nvr = C.sbuf("nvr", [1, 16 * 64], BF16)
    C.LW.inc(nc.gpsimd.dma_start(nkT[:, :], C.win("w_nullkT")[:, :]), 16)
    C.LW.inc(nc.gpsimd.dma_start(nvr[:, :], C.win("w_nullvr")[:, :]), 16)
    onesneg = C.sbuf("onesneg", [1, 128], BF16)
    onesr64 = C.sbuf("onesr64", [1, 64], F32)
    nc.vector.memset(onesneg[:, :], -1.0)
    nc.vector.memset(onesr64[:, :], 1.0)
    kh = [C.sbuf("kh3", [64, T], BF16) for _ in range(2)]
    qh = [C.sbuf("qh3", [64, T], BF16) for _ in range(2)]
    va = [C.sbuf("va3", [128, 32 * 65], BF16) for _ in range(2)]
    vna = C.sbuf("vna", [1, 65], BF16)
    eb = [C.sbuf("eb3", [128, TC], BF16) for _ in range(2)]
    ebn = C.sbuf("ebn3", [1, TC], BF16)
    psS = [C.psum("psS3", [128, TC]) for _ in range(2)]
    psN = C.psum("psN3", [1, TC])
    psO = C.psum("psO3", [65, TC])
    psR = C.psum("psR3", [64, TC])
    psP = [C.psum("psP3", [128, TC]) for _ in range(2)]
    psM = C.psum("psM3", [32, 128])
    rec = C.sbuf("rec3", [1, TC], F32)
    osb = C.sbuf("osb3", [64, TC], F32)
    ob = [C.sbuf("obh3", [64, T], BF16) for _ in range(2)]
    mtmp = C.sbuf("mtmp", [128, TC], F32)
    mred = C.sbuf("mred", [128, 1], F32)
    mcols = C.sbuf("mcols", [128, 32], F32)
    m32 = C.sbuf("m32", [32, 128], F32)
    mrow = C.sbuf("mrow", [1, T], BF16)

    def load_head(h):
        p = h % 2
        C.LD.inc(nc.sync.dma_start(kh[p][:, :], kd[h * 64:(h + 1) * 64, :]), 16)
        C.LD.inc(nc.sync.dma_start(qh[p][:, :], qd[h * 64:(h + 1) * 64, :]), 16)
        C.LD.inc(nc.sync.dma_start(va[p].rearrange("p (n e) -> p n e", e=65)[:, :, 0:64],
                                   vd[:, h * 64:(h + 1) * 64].rearrange("(n p) d -> p n d", p=128)), 16)

    load_head(0)
    ld_at = {0: C.LD.v}
    pe_done = {}
    st_done = {}
    for h in range(16):
        p = h % 2
        if h + 1 < 16:
            if h >= 1:
                nc.sync.wait_ge(C.PE.h, pe_done[h - 1])
            load_head(h + 1)
            ld_at[h + 1] = C.LD.v
        nc.vector.wait_ge(C.LD.h, C.LD.v)
        nc.vector.wait_ge(C.LW.h, C.LW.v)
        if h >= 2:
            nc.vector.wait_ge(C.ST.h, C.ST.v)
        C.DVE.inc(nc.vector.memset(va[p].rearrange("p (n e) -> p n e", e=65)[:, :, 64:65], 1.0), 1)
        C.DVE.inc(nc.vector.tensor_copy(vna[0:1, 0:64], nvr[0:1, h * 64:(h + 1) * 64]), 1)
        C.DVE.inc(nc.vector.memset(vna[0:1, 64:65], 1.0), 1)
        nc.tensor.wait_ge(C.DVE.h, C.DVE.v)
        for tt in range(32):
            nch = tt // 4 + 1
            for ck in range(nch):
                sb = ck % 2
                mi = nc.tensor.matmul(psP[sb][:, :], qh[p][:, tt * 128:(tt + 1) * 128],
                                      kh[p][:, ck * TC:(ck + 1) * TC], start=True, stop=True)
                C.PE.inc(mi, 1)
                nc.vector.wait_ge(C.PE.h, C.PE.v)
                if ck == tt // 4:
                    s2o = ck * TC - tt * 128
                    nc.vector.tensor_add(mtmp[:, :], psP[sb][:, :], am[:, 512 + s2o: 1024 + s2o])
                    nc.vector.tensor_reduce(mred[:, :], mtmp[:, :], AX.X, ALU.max)
                else:
                    nc.vector.tensor_reduce(mred[:, :], psP[sb][:, :], AX.X, ALU.max)
                if ck == 0:
                    i = nc.vector.tensor_copy(mcols[:, tt:tt + 1], mred[:, :])
                else:
                    i = nc.vector.tensor_max(mcols[:, tt:tt + 1], mcols[:, tt:tt + 1], mred[:, :])
                C.DVE.inc(i, 1)
                nc.tensor.wait_ge(C.DVE.h, C.DVE.v)
            mi = nc.tensor.matmul(psP[tt % 2][:, 0:1], qh[p][:, tt * 128:(tt + 1) * 128],
                                  nkT[:, h:h + 1], start=True, stop=True)
            C.PE.inc(mi, 1)
            nc.vector.wait_ge(C.PE.h, C.PE.v)
            i = nc.vector.tensor_max(mcols[:, tt:tt + 1], mcols[:, tt:tt + 1], psP[tt % 2][:, 0:1])
            C.DVE.inc(i, 1)
            nc.tensor.wait_ge(C.DVE.h, C.DVE.v)
        C.PE.inc(nc.tensor.transpose(psM[:, :], mcols[:, :], ids[:, :]), 1)
        nc.vector.wait_ge(C.PE.h, C.PE.v)
        C.DVE.inc(nc.vector.tensor_copy(m32[:, :], psM[:, :]), 1)
        nc.sync.wait_ge(C.DVE.h, C.DVE.v)
        C.ST.inc(nc.sync.dma_start(md[h, :].rearrange("(n p) -> n p", n=32), m32[:, :]), 16)
        nc.gpsimd.wait_ge(C.ST.h, C.ST.v)
        C.LW.inc(nc.gpsimd.dma_start(mrow[0:1, :], md[h:h + 1, :]), 16)
        nc.tensor.wait_ge(C.LW.h, C.LW.v)
        for c in range(CT):
            ntk = 4 * (c + 1)
            exp_done = [None, None]
            for j in range(ntk):
                sb = j % 2
                if exp_done[sb] is not None:
                    nc.tensor.wait_ge(C.ACT.h, exp_done[sb])
                nc.tensor.matmul(psS[sb][:, :], kh[p][:, j * 128:(j + 1) * 128],
                                 qh[p][:, c * TC:(c + 1) * TC], start=True, stop=False)
                mi = nc.tensor.matmul(psS[sb][:, :], onesneg[:, :], mrow[0:1, c * TC:(c + 1) * TC],
                                      start=False, stop=True)
                C.PE.inc(mi, 1)
                diag = j >= ntk - 4
                if diag:
                    s3 = j * 128 - c * TC
                    nc.vector.wait_ge(C.PE.h, C.PE.v)
                    C.DVE.inc(nc.vector.tensor_add(psS[sb][:, :], psS[sb][:, :],
                                                   am2[:, 512 - s3:1024 - s3]), 1)
                    nc.scalar.wait_ge(C.DVE.h, C.DVE.v)
                else:
                    nc.scalar.wait_ge(C.PE.h, C.PE.v)
                C.ACT.inc(nc.scalar.activation(eb[sb][:, :], psS[sb][:, :], AF.Exp), 1)
                exp_done[sb] = C.ACT.v
                nc.tensor.wait_ge(C.ACT.h, C.ACT.v)
                nc.tensor.matmul(psO[:, :], va[p][:, j * 65:(j + 1) * 65], eb[sb][:, :],
                                 start=(j == 0), stop=False)
            nc.tensor.matmul(psN[:, :], nkT[:, h:h + 1], qh[p][:, c * TC:(c + 1) * TC],
                             start=True, stop=False)
            mi = nc.tensor.matmul(psN[:, :], onesneg[0:1, 0:1], mrow[0:1, c * TC:(c + 1) * TC],
                                  start=False, stop=True)
            C.PE.inc(mi, 1)
            nc.scalar.wait_ge(C.PE.h, C.PE.v)
            C.ACT.inc(nc.scalar.activation(ebn[:, :], psN[:, :], AF.Exp), 1)
            nc.tensor.wait_ge(C.ACT.h, C.ACT.v)
            mi = nc.tensor.matmul(psO[:, :], vna[:, :], ebn[:, :], start=False, stop=True)
            C.PE.inc(mi, 1)
            nc.vector.wait_ge(C.PE.h, C.PE.v)
            nc.vector.reciprocal(rec[:, :], psO[64:65, :])
            C.DVE.inc(nc.vector.tensor_copy(osb[:, :], psO[0:64, :]), 1)
            nc.tensor.wait_ge(C.DVE.h, C.DVE.v)
            C.PE.inc(nc.tensor.matmul(psR[:, :], onesr64[:, :], rec[:, :], start=True, stop=True), 1)
            nc.vector.wait_ge(C.PE.h, C.PE.v)
            C.DVE.inc(nc.vector.tensor_mul(ob[p][:, c * TC:(c + 1) * TC], osb[:, :], psR[:, :]), 1)
            nc.tensor.wait_ge(C.DVE.h, C.DVE.v)
        pe_done[h] = C.PE.v
        nc.sync.wait_ge(C.DVE.h, C.DVE.v)
        C.ST.inc(nc.sync.dma_start(out[h * 64:(h + 1) * 64, :], ob[p][:, :]), 16)
        st_done[h] = C.ST.v

def st_wo3(C):
    of = load_packed(C, C.din("of3"), D, T, name="ofsb3")
    st_proj_T(C, w="L3_attn_Wo", src_sb=of, out_name="s3", tag="wo3")

def st_rms3(C):
    nc = C.nc
    src = C.din("xn3")
    dst = C.dout("t3")
    nsc = load_cols(C, C.win("L3_ffn_norm_scale"), D, name="nsc")
    xs = load_packed(C, src, D, T, name="rx")
    ones = C.sbuf("r_ones", [128, 1], F32)
    onesr = C.sbuf("r_onesr", [1, 128], F32)
    sq = C.sbuf("r_sq", [128, RT * TC], F32)
    ps_q = C.psum("r_pq", [1, TC])
    rs = C.sbuf("r_rs", [1, TC], F32)
    rb = C.psum("r_rb", [128, TC])
    xo = C.sbuf("r_xo", [128, RT * TC], BF16)
    nc.vector.memset(ones[:, :], 1.0)
    C.DVE.inc(nc.vector.memset(onesr[:, :], 1.0), 1)
    nc.tensor.wait_ge(C.DVE.h, C.DVE.v)
    nc.scalar.wait_ge(C.LW.h, C.LW.v)
    for c in range(CT):
        nc.scalar.wait_ge(C.PE.h, C.PE.v)
        for r in range(RT):
            si = nc.scalar.activation(sq[:, r * TC:(r + 1) * TC],
                                      xs[:, r * T + c * TC: r * T + (c + 1) * TC], AF.Square)
        C.ACT.inc(si, 1)
        nc.tensor.wait_ge(C.ACT.h, C.ACT.v)
        nc.tensor.wait_ge(C.DVE.h, C.DVE.v)
        for r in range(RT):
            mi = nc.tensor.matmul(ps_q[:, :], ones[:, :], sq[:, r * TC:(r + 1) * TC],
                                  start=(r == 0), stop=(r == RT - 1))
        C.PE.inc(mi, 1)
        nc.scalar.wait_ge(C.PE.h, C.PE.v)
        C.ACT.inc(nc.scalar.activation(rs[:, :], ps_q[:, :], AF.Sqrt, scale=1.0 / D,
                                       bias=C.cbias(1e-5, p=1)), 1)
        nc.vector.wait_ge(C.ACT.h, C.ACT.v)
        C.DVE.inc(nc.vector.reciprocal(rs[:, :], rs[:, :]), 1)
        nc.tensor.wait_ge(C.DVE.h, C.DVE.v)
        mi = nc.tensor.matmul(rb[:, :], onesr[:, :], rs[:, :], start=True, stop=True)
        C.PE.inc(mi, 1)
        nc.vector.wait_ge(C.PE.h, C.PE.v)
        for r in range(RT):
            i = nc.vector.scalar_tensor_tensor(xo[:, r * TC:(r + 1) * TC],
                                               xs[:, r * T + c * TC: r * T + (c + 1) * TC],
                                               nsc[:, r:r + 1], rb[:, :], ALU.mult, ALU.mult)
        C.DVE.inc(i, 1)
        nc.sync.wait_ge(C.DVE.h, C.DVE.v)
        C.ST.inc(nc.sync.dma_start(dst[:, c * TC:(c + 1) * TC].rearrange("(r p) t -> p r t", p=128),
                                   xo.rearrange("p (r t) -> p r t", t=TC)), 16)
        nc.vector.wait_ge(C.ST.h, C.ST.v)

def vedge(C, instr=None):
    if instr is None:
        instr = C.nc.vector.nop()
    C.DVE.inc(instr, 1)
    C.nc.vector.wait_ge(C.DVE.h, C.DVE.v)

def topk_combine(C, X, E, K, iot, iotB, scr):
    nc = C.nc
    masks = scr["masks"]
    mcol = scr["mcol"]
    wcol = scr["wcol"]
    tpos = scr["tpos"]
    comb = scr["comb"]
    den = scr["den"]
    for i in range(K):
        vedge(C, nc.vector.tensor_reduce(mcol[:, i:i + 1], X[:, :], AX.X, ALU.max))
        vedge(C, nc.vector.tensor_scalar(tpos[:, :], X[:, :], mcol[:, i:i + 1], None, ALU.is_ge))
        vedge(C, nc.vector.scalar_tensor_tensor(tpos[:, :], tpos[:, :], -1e6, iotB[:, 0:E],
                                                ALU.mult, ALU.add))
        vedge(C, nc.vector.tensor_reduce(den[:, :], tpos[:, :], AX.X, ALU.min))
        vedge(C, nc.vector.tensor_scalar(masks[:, i * E:(i + 1) * E], iot[:, 0:E], den[:, :],
                                         None, ALU.is_equal))
        if i + 1 < K:
            vedge(C, nc.vector.scalar_tensor_tensor(X[:, :], masks[:, i * E:(i + 1) * E], -1e9,
                                                    X[:, :], ALU.mult, ALU.add))
    nc.scalar.wait_ge(C.DVE.h, C.DVE.v)
    si = nc.scalar.activation(wcol[:, 0:K], mcol[:, 0:K], AF.Exp, bias=scr["negm1"], scale=1.0)
    C.ACT.inc(si, 1)
    nc.vector.wait_ge(C.ACT.h, C.ACT.v)
    vedge(C, nc.vector.tensor_reduce(den[:, :], wcol[:, 0:K], AX.X, ALU.add))
    vedge(C, nc.vector.reciprocal(den[:, :], den[:, :]))
    vedge(C, nc.vector.tensor_scalar_mul(wcol[:, 0:K], wcol[:, 0:K], den[:, :]))
    vedge(C, nc.vector.tensor_scalar_mul(comb[:, :], masks[:, 0:E], wcol[:, 0:1]))
    for i in range(1, K):
        vedge(C, nc.vector.scalar_tensor_tensor(comb[:, :], masks[:, i * E:(i + 1) * E],
                                                wcol[:, i:i + 1], comb[:, :], ALU.mult, ALU.add))
    ii = nc.vector.nop()
    return ii

def st_gatecomb(C, l):
    nc = C.nc
    if l == 3:
        E, K = 32, 4
        src_sb = load_packed(C, C.din("t3"), D, T, name="gc_t")
        wname = "w_gate3T"
        gb = load_cols(C, C.win("L3_ffn_gate_b"), 32, name="gb3")
        dst = C.dout("comb3")
    else:
        E, K = 8, 2
        src_sb = load_packed(C, C.din("xn2"), D, T, name="gc_x")
        wname = "L2_ffn_Wg"
        gb = None
        dst = C.dout("comb2")
    w = C.win(wname)
    srcd = C.din("t3" if l == 3 else "xn2")
    glsb = C.sbuf("glsb", [E, T], F32)
    ids = C.sbuf("gid", [128, 128], F32)
    iot = C.sbuf("giot", [128, 32], F32)
    iotB = C.sbuf("giotB", [128, 32], F32)
    C.LW.inc(nc.gpsimd.dma_start(ids[:, :], C.win("c_ident")[:, :]), 16)
    C.LW.inc(nc.gpsimd.dma_start(iot[:, :], C.win("c_iota32")[:, :]), 16)
    C.LW.inc(nc.gpsimd.dma_start(iotB[:, :], C.win("c_iotaB32")[:, :]), 16)
    wsb = C.sbuf("gwf", [128, RT * E], F32)
    C.LD.inc(nc.sync.dma_start(wsb.rearrange("p (r e) -> p r e", e=E),
                               w.rearrange("(r p) e -> p r e", p=128)), 16)
    xf = [C.sbuf("gxf", [128, RT * TC], F32) for _ in range(2)]
    psg = [C.psum("gpsg", [E, TC]) for _ in range(2)]
    for c in range(CT):
        nc.gpsimd.wait_ge(C.PE.h, C.PE.v)
        C.LW.inc(nc.gpsimd.dma_start(xf[c % 2].rearrange("p (r t) -> p r t", t=TC),
                                     srcd[:, c * TC:(c + 1) * TC].rearrange("(r p) t -> p r t", p=128)), 16)
        nc.tensor.wait_ge(C.LW.h, C.LW.v)
        nc.tensor.wait_ge(C.LD.h, C.LD.v)
        if c >= 2:
            nc.tensor.wait_ge(C.DVE.h, C.DVE.v)
        for r in range(RT):
            mi = nc.tensor.matmul(psg[c % 2][:, :], wsb[:, r * E:(r + 1) * E],
                                  xf[c % 2][:, r * TC:(r + 1) * TC],
                                  start=(r == 0), stop=(r == RT - 1))
        C.PE.inc(mi, 1)
        nc.vector.wait_ge(C.PE.h, C.PE.v)
        if gb is not None:
            i = nc.vector.tensor_scalar(glsb[:, c * TC:(c + 1) * TC], psg[c % 2][:, :],
                                        gb[0:E, 0:1], None, ALU.add)
        else:
            i = nc.vector.tensor_copy(glsb[:, c * TC:(c + 1) * TC], psg[c % 2][:, :])
        C.DVE.inc(i, 1)
    C.free(*xf, *psg)
    psT = C.psum("gpsT", [128, 32])
    psB = C.psum("gpsB", [E, 128])
    X = C.sbuf("gX", [128, 32], F32)
    scr = {
        "masks": C.sbuf("gmask", [128, E * K], F32),
        "mcol": C.sbuf("gmcol", [128, K], F32),
        "wcol": C.sbuf("gwcol", [128, K], F32),
        "tpos": C.sbuf("gtpos", [128, E], F32),
        "comb": C.sbuf("gcomb", [128, E], F32),
        "den": C.sbuf("gden", [128, 1], F32),
    }
    scr["negm1"] = scr["mcol"][:, 0:1]
    negm = C.sbuf("gnegm", [128, 1], F32)
    scr["negm1"] = negm
    csb = C.sbuf("gcsb", [E, T], F32)
    for tt in range(32):
        nc.tensor.wait_ge(C.DVE.h, C.DVE.v)
        nc.tensor.wait_ge(C.ACT.h, C.ACT.v)
        C.PE.inc(nc.tensor.transpose(psT[:, 0:E], glsb[:, tt * 128:(tt + 1) * 128], ids[0:E, 0:E]), 1)
        nc.vector.wait_ge(C.PE.h, C.PE.v)
        C.DVE.inc(nc.vector.tensor_copy(X[:, 0:E], psT[:, 0:E]), 1)
        i0 = nc.vector.tensor_reduce(negm[:, :], X[:, 0:E], AX.X, ALU.max)
        vedge(C, i0)
        i0 = nc.vector.tensor_scalar_mul(negm[:, :], negm[:, :], -1.0)
        vedge(C, i0)
        ii = topk_combine(C, X[:, 0:E], E, K, iot, iotB, scr)
        C.DVE.inc(ii, 1)
        nc.tensor.wait_ge(C.DVE.h, C.DVE.v)
        C.PE.inc(nc.tensor.transpose(psB[:, :], scr["comb"][:, 0:E], ids[:, :]), 1)
        nc.vector.wait_ge(C.PE.h, C.PE.v)
        C.DVE.inc(nc.vector.tensor_copy(csb[:, tt * 128:(tt + 1) * 128], psB[0:E, :]), 1)
    nc.sync.wait_ge(C.DVE.h, C.DVE.v)
    C.ST.inc(nc.sync.dma_start(dst[:, :], csb[0:E, :]), 16)

F3E = 32
F3NWIN = 4

def st_ffn3(C):
    nc = C.nc
    E = F3E
    WIN = 1024
    NWIN = F3NWIN
    wch = WIN // TC
    t3sb = load_packed(C, C.din("t3"), D, T, name="f3t")
    combd = C.din("comb3")
    xnd = C.din("xn3")
    out = C.dout("g3")
    w1 = C.win("w_l3w1T")
    w2 = C.win("w_l3w2T")
    b1 = C.win("w_l3b1p")
    b2 = C.win("L3_ffn_b2")
    combsb = C.sbuf("f3comb", [32, T], F32)
    C.LD.inc(nc.sync.dma_start(combsb[:, :], combd[:, :]), 16)
    b2sb = C.sbuf("f3b2", [32, D], F32)
    C.LD.inc(nc.sync.dma_start(b2sb[:, :], b2[:, :]), 16)
    crow = C.sbuf("f3crow", [1, T], BF16)
    glusb = C.sbuf("f3glu", [128, 8 * WIN], BF16)
    hsw = C.sbuf("f3hsw", [128, 8 * WIN], BF16)
    outsb = C.sbuf("f3out", [128, 8 * WIN], F32)
    sg = C.sbuf("f3sg", [128, TC], F32)
    lin2 = C.sbuf("f3lin", [128, TC], F32)
    t2 = C.sbuf("f3t2", [128, TC], F32)
    cbb = C.psum("f3cbb", [128, TC])
    psb2 = C.psum("f3psb2", [128, TC])
    onesr = C.sbuf("f3ones", [1, 128], BF16)
    nc.vector.memset(onesr[:, :], 1.0)
    xnw = C.sbuf("f3xn", [128, RT * TC], F32)
    ob = C.sbuf("f3ob", [128, RT * TC], F32)
    for wi in range(NWIN):
        chunks = [wi * wch + j for j in range(wch)]
        nc.vector.wait_ge(C.ST.h, C.ST.v)
        C.DVE.inc(nc.vector.memset(outsb[:, :], 0.0), 1)
        for e in range(E):
            b1c = load_cols(C, b1[e, :], 2048, name="f3b1c")
            nc.gpsimd.wait_ge(C.PE.h, C.PE.v)
            C.LW.inc(nc.gpsimd.dma_start(crow[0:1, :], combd[e:e + 1, :]), 16)

            def cb1(mt, c, ps):
                cl = c - wi * wch
                nc.vector.wait_ge(C.PE.h, C.PE.v)
                if mt < 8:
                    i = nc.vector.tensor_scalar(
                        glusb[:, mt * WIN + cl * TC: mt * WIN + (cl + 1) * TC],
                        ps, b1c[:, mt:mt + 1], 7.0, ALU.add, ALU.min)
                else:
                    m8 = mt - 8
                    vedge(C, nc.vector.tensor_scalar(lin2[:, :], ps, b1c[:, mt:mt + 1], 7.0, ALU.add, ALU.min))
                    vedge(C, nc.vector.tensor_scalar(lin2[:, :], lin2[:, :], -7.0, None, ALU.max))
                    vedge(C, nc.vector.tensor_scalar(lin2[:, :], lin2[:, :], 1.0, None, ALU.add))
                    gsl = glusb[:, m8 * WIN + cl * TC: m8 * WIN + (cl + 1) * TC]
                    nc.scalar.wait_ge(C.DVE.h, C.DVE.v)
                    C.ACT.inc(nc.scalar.activation(sg[:, :], gsl, AF.Sigmoid, scale=1.702), 1)
                    nc.vector.wait_ge(C.ACT.h, C.ACT.v)
                    nc.vector.tensor_mul(t2[:, :], sg[:, :], lin2[:, :])
                    nc.vector.tensor_mul(t2[:, :], t2[:, :], gsl)
                    nc.tensor.wait_ge(C.LW.h, C.LW.v)
                    nc.tensor.wait_ge(C.DVE.h, C.DVE.v)
                    C.PE.inc(nc.tensor.matmul(cbb[:, :], onesr[:, :],
                                              crow[0:1, c * TC:(c + 1) * TC], start=True, stop=True), 1)
                    nc.vector.wait_ge(C.PE.h, C.PE.v)
                    i = nc.vector.tensor_mul(
                        hsw[:, m8 * WIN + cl * TC: m8 * WIN + (cl + 1) * TC], t2[:, :], cbb[:, :])
                return i

            gemm_T(C, terms=[("dram", w1[e], t3sb)], din=D, dout=2048, out_cb=cb1,
                   tag="f3a", nbanks=4, wait_first=True, chunks=chunks)

            def cb2(mt, c, ps):
                nc.vector.wait_ge(C.PE.h, C.PE.v)
                sl = outsb[:, mt * WIN + c * TC: mt * WIN + (c + 1) * TC]
                return nc.vector.tensor_add(sl, sl, ps)

            gemm_T(C, terms=[("dram", w2[e], hsw)], din=D, dout=D, out_cb=cb2,
                   tag="f3b", nbanks=4, wait_first=True, nT=WIN,
                   chunks=list(range(wch)))
            C.free(b1c)
        for cl in range(wch):
            c = wi * wch + cl
            nc.gpsimd.wait_ge(C.DVE.h, C.DVE.v)
            C.LW.inc(nc.gpsimd.dma_start(xnw.rearrange("p (r t) -> p r t", t=TC),
                                         xnd[:, c * TC:(c + 1) * TC].rearrange("(r p) t -> p r t", p=128)), 16)
            ld_here = C.LW.v
            for r in range(RT):
                nc.tensor.wait_ge(C.DVE.h, C.DVE.v)
                mi = nc.tensor.matmul(psb2[:, :], b2sb[:, r * 128:(r + 1) * 128],
                                      combsb[:, c * TC:(c + 1) * TC], start=True, stop=True)
                C.PE.inc(mi, 1)
                nc.vector.wait_ge(C.PE.h, C.PE.v)
                nc.vector.wait_ge(C.LW.h, ld_here)
                sl = outsb[:, r * WIN + cl * TC: r * WIN + (cl + 1) * TC]
                nc.vector.tensor_add(sl, sl, psb2[:, :])
                C.DVE.inc(nc.vector.tensor_add(ob[:, r * TC:(r + 1) * TC], sl,
                                               xnw[:, r * TC:(r + 1) * TC]), 1)
                nc.scalar.wait_ge(C.DVE.h, C.DVE.v)
                C.ACT.inc(nc.scalar.activation(ob[:, r * TC:(r + 1) * TC],
                                               ob[:, r * TC:(r + 1) * TC], AF.Silu), 1)
            nc.sync.wait_ge(C.ACT.h, C.ACT.v)
            C.ST.inc(nc.sync.dma_start(out[:, c * TC:(c + 1) * TC].rearrange("(r p) t -> p r t", p=128),
                                       ob.rearrange("p (r t) -> p r t", t=TC)), 16)
            nc.vector.wait_ge(C.ST.h, C.ST.v)


def st_spec2(C):
    xn_sb = load_packed(C, C.din("xn2"), D, T, name="xnsb2")
    st_proj_T(C, w="w_L2WinT", src_sb=xn_sb, out_name="u2", tag="sp2")

def st_scan2(C):
    nc = C.nc
    u2d = C.din("u2")
    z = [[[C.sbuf(f"z{comp}{t}{par}", [128, T], F32) for par in range(2)]
          for t in range(2)] for comp in range(2)]
    for t in range(2):
        C.LD.inc(nc.sync.dma_start(z[0][t][0][:, :], u2d[t * 128:(t + 1) * 128, :]), 16)
        C.LD.inc(nc.sync.dma_start(z[1][t][0][:, :], u2d[256 + t * 128: 256 + (t + 1) * 128, :]), 16)
    ldc = load_cols(C, C.win("L2_attn_log_decay"), 256, name="ldc")
    fqc = load_cols(C, C.win("L2_attn_frequencies"), 256, name="fqc")
    par = C.sbuf("scpar", [128, 32], F32)
    nc.vector.wait_ge(C.LW.h, C.LW.v)
    nc.scalar.wait_ge(C.LW.h, C.LW.v)
    for t in range(2):
        dec = par[:, 10 + t:11 + t]
        m = par[:, 12 + t:13 + t]
        s = par[:, 14 + t:15 + t]
        c2 = par[:, 16 + t:17 + t]
        tmp = par[:, 18 + t:19 + t]
        t2c = par[:, 20 + t:21 + t]
        t3c = par[:, 22 + t:23 + t]
        omc = par[:, 24 + t:25 + t]
        s2c = par[:, 26 + t:27 + t]
        C.ACT.inc(nc.scalar.activation(dec, ldc[:, t:t + 1], AF.Sigmoid), 1)
        nc.vector.wait_ge(C.ACT.h, C.ACT.v)
        vedge(C, nc.vector.tensor_scalar(tmp, fqc[:, t:t + 1], float(0.1 / (2 * np.pi)), None, ALU.mult))
        vedge(C, nc.vector.tensor_scalar(t2c, tmp, 12582912.0, None, ALU.add))
        vedge(C, nc.vector.tensor_scalar(t3c, t2c, 12582912.0, None, ALU.subtract))
        vedge(C, nc.vector.tensor_scalar(omc, fqc[:, t:t + 1], 0.1, None, ALU.mult))
        vedge(C, nc.vector.scalar_tensor_tensor(m, t3c, float(-2 * np.pi), omc, ALU.mult, ALU.add))
        nc.scalar.wait_ge(C.DVE.h, C.DVE.v)
        nc.scalar.activation(s, m, AF.Sin, scale=0.5)
        C.ACT.inc(nc.scalar.activation(c2, m, AF.Sin, scale=-0.5, bias=C.cbias(float(np.pi / 2))), 1)
        nc.vector.wait_ge(C.ACT.h, C.ACT.v)
        rotr = par[:, 6 + t:7 + t]
        roti = par[:, 8 + t:9 + t]
        vedge(C, nc.vector.tensor_mul(s2c, s, s))
        vedge(C, nc.vector.scalar_tensor_tensor(rotr, s2c, -2.0, C.cbias(1.0), ALU.mult, ALU.add))
        vedge(C, nc.vector.tensor_mul(roti, s, c2))
        vedge(C, nc.vector.tensor_scalar(roti, roti, 2.0, None, ALU.mult))
        vedge(C, nc.vector.tensor_mul(par[:, 0 + t:1 + t], dec, rotr))
        vedge(C, nc.vector.tensor_mul(par[:, 2 + t:3 + t], dec, roti))
        vedge(C, nc.vector.tensor_scalar(par[:, 4 + t:5 + t], par[:, 2 + t:3 + t], -1.0,
                                         None, ALU.mult))
    nc.vector.wait_ge(C.LD.h, C.LD.v)
    p = 0
    for step in range(12):
        d = 1 << step
        for t in range(2):
            Ar = par[:, 0 + t:1 + t]
            Ai = par[:, 2 + t:3 + t]
            nAi = par[:, 4 + t:5 + t]
            sr, si_ = z[0][t][p], z[1][t][p]
            dr, di = z[0][t][1 - p], z[1][t][1 - p]
            vedge(C, nc.vector.scalar_tensor_tensor(dr[:, d:], sr[:, 0:T - d], Ar, sr[:, d:], ALU.mult, ALU.add))
            vedge(C, nc.vector.scalar_tensor_tensor(dr[:, d:], si_[:, 0:T - d], nAi, dr[:, d:], ALU.mult, ALU.add))
            vedge(C, nc.vector.scalar_tensor_tensor(di[:, d:], si_[:, 0:T - d], Ar, si_[:, d:], ALU.mult, ALU.add))
            vedge(C, nc.vector.scalar_tensor_tensor(di[:, d:], sr[:, 0:T - d], Ai, di[:, d:], ALU.mult, ALU.add))
            vedge(C, nc.vector.tensor_copy(dr[:, 0:d], sr[:, 0:d]))
            vedge(C, nc.vector.tensor_copy(di[:, 0:d], si_[:, 0:d]))
            tmp = par[:, 18 + t:19 + t]
            tmp2 = par[:, 20 + t:21 + t]
            vedge(C, nc.vector.tensor_mul(tmp, Ar, Ar))
            vedge(C, nc.vector.tensor_mul(tmp2, Ai, Ai))
            vedge(C, nc.vector.tensor_mul(Ai, Ar, Ai))
            vedge(C, nc.vector.tensor_scalar(Ai, Ai, 2.0, None, ALU.mult))
            vedge(C, nc.vector.tensor_sub(Ar, tmp, tmp2))
            vedge(C, nc.vector.tensor_scalar(nAi, Ai, -1.0, None, ALU.mult))
        p = 1 - p
    featsb = C.sbuf("feats", [128, 4 * T], BF16)
    for t in range(2):
        rotr = par[:, 6 + t:7 + t]
        roti = par[:, 8 + t:9 + t]
        nroti = par[:, 18 + t:19 + t]
        vedge(C, nc.vector.tensor_scalar(nroti, roti, -1.0, None, ALU.mult))
        zr, zi = z[0][t][p], z[1][t][p]
        s1 = z[0][t][1 - p]
        s2 = z[1][t][1 - p]
        fre = featsb[:, t * T:(t + 1) * T]
        fim = featsb[:, (2 + t) * T:(3 + t) * T]
        vedge(C, nc.vector.tensor_scalar(s1[:, :], zr[:, :], rotr, None, ALU.mult))
        nc.vector.scalar_tensor_tensor(fre, zi[:, :], nroti, s1[:, :], ALU.mult, ALU.add)
        vedge(C, nc.vector.tensor_scalar(s2[:, :], zi[:, :], rotr, None, ALU.mult))
        i = nc.vector.scalar_tensor_tensor(fim, zr[:, :], roti, s2[:, :], ALU.mult, ALU.add)
        C.DVE.inc(i, 1)
    st_proj_T(C, w="w_L2WoutT", src_sb=featsb, out_name="s2", tag="wo2", din=512, dout=D)

def st_ffn2(C):
    nc = C.nc
    E = 8
    xnsb = load_packed(C, C.din("xn2"), D, T, name="f2x")
    combd = C.din("comb2")
    out = C.dout("g2")
    W1 = C.win("L2_ffn_W1")
    W2 = C.win("L2_ffn_W2")
    b1 = C.win("L2_ffn_b1")
    b2 = C.win("L2_ffn_b2")
    Ws1 = C.win("L2_ffn_Ws1")
    bs1 = C.win("L2_ffn_bs1")
    Ws2 = C.win("L2_ffn_Ws2")
    bs2 = C.win("L2_ffn_bs2")
    combsb = C.sbuf("f2comb", [8, T], F32)
    C.LD.inc(nc.sync.dma_start(combsb[:, :], combd[:, :]), 16)
    b2sb = C.sbuf("f2b2", [8, D], F32)
    C.LD.inc(nc.sync.dma_start(b2sb[:, :], b2[:, :]), 16)
    bs2c = load_cols(C, bs2, D, name="f2bs2")
    crow = C.sbuf("f2crow", [1, T], BF16)
    hsb = C.sbuf("f2h", [128, 32 * TC], BF16)
    outsb = C.sbuf("f2out", [128, 8 * TC], F32)
    cbbs = C.sbuf("f2cbb", [128, TC], F32)
    cbp = C.psum("f2cbp", [128, TC])
    psb2 = C.psum("f2psb2", [128, TC])
    onesr = C.sbuf("f2ones", [1, 128], BF16)
    tmp = C.sbuf("f2tmp", [128, TC], F32)
    ob = C.sbuf("f2ob", [128, RT * TC], F32)
    nc.vector.memset(onesr[:, :], 1.0)
    for c in range(CT):
        nc.vector.wait_ge(C.ST.h, C.ST.v)
        C.DVE.inc(nc.vector.memset(outsb[:, :], 0.0), 1)
        for e in range(E + 1):
            shared = e == E
            if not shared:
                b1c = load_cols(C, b1[e, :], 4096, name="f2b1c")
                nc.gpsimd.wait_ge(C.PE.h, C.PE.v)
                C.LW.inc(nc.gpsimd.dma_start(crow[0:1, :], combd[e:e + 1, :]), 16)
                lw_crow = C.LW.v
                w1t, w2t = W1[e], W2[e]
            else:
                b1c = load_cols(C, bs1, 4096, name="f2b1c")
                w1t, w2t = Ws1, Ws2
            act_guard = [0]

            def cb1(mt, c2, ps):
                nc.scalar.wait_ge(C.PE.h, C.PE.v)
                nc.scalar.wait_ge(C.LW.h, C.LW.v)
                i = nc.scalar.activation(hsb[:, mt * TC:(mt + 1) * TC], ps, AF.Relu,
                                         bias=b1c[:, mt:mt + 1])
                return i

            gemm_T(C, terms=[("dram", w1t, xnsb)], din=D, dout=4096, out_cb=cb1,
                   tag="f2a", nbanks=4, wait_first=True, chunks=[c])
            if not shared:
                nc.tensor.wait_ge(C.LW.h, C.LW.v)
                nc.tensor.wait_ge(C.DVE.h, C.DVE.v)
                C.PE.inc(nc.tensor.matmul(cbp[:, :], onesr[:, :], crow[0:1, c * TC:(c + 1) * TC],
                                          start=True, stop=True), 1)
                nc.vector.wait_ge(C.PE.h, C.PE.v)
                C.DVE.inc(nc.vector.tensor_copy(cbbs[:, :], cbp[:, :]), 1)

            def cb2(mt, c2, ps):
                nc.vector.wait_ge(C.PE.h, C.PE.v)
                sl = outsb[:, mt * TC:(mt + 1) * TC]
                if shared:
                    vedge(C, nc.vector.tensor_scalar(tmp[:, :], ps, bs2c[:, mt:mt + 1], None, ALU.add))
                    i = nc.vector.tensor_add(sl, sl, tmp[:, :])
                else:
                    nc.vector.tensor_mul(tmp[:, :], ps, cbbs[:, :])
                    i = nc.vector.tensor_add(sl, sl, tmp[:, :])
                return i

            gemm_T(C, terms=[("dram", w2t, hsb)], din=4096, dout=D, out_cb=cb2,
                   tag="f2b", nbanks=4, wait_first=True, nT=TC, cw=TC, chunks=[0])
            C.free(b1c)
        for r in range(RT):
            nc.tensor.wait_ge(C.DVE.h, C.DVE.v)
            mi = nc.tensor.matmul(psb2[:, :], b2sb[:, r * 128:(r + 1) * 128],
                                  combsb[:, c * TC:(c + 1) * TC], start=True, stop=True)
            C.PE.inc(mi, 1)
            nc.vector.wait_ge(C.PE.h, C.PE.v)
            sl = outsb[:, r * TC:(r + 1) * TC]
            C.DVE.inc(nc.vector.tensor_add(sl, sl, psb2[:, :]), 1)
            nc.scalar.wait_ge(C.DVE.h, C.DVE.v)
            C.ACT.inc(nc.scalar.activation(ob[:, r * TC:(r + 1) * TC], sl, AF.Silu), 1)
        nc.sync.wait_ge(C.ACT.h, C.ACT.v)
        C.ST.inc(nc.sync.dma_start(out[:, c * TC:(c + 1) * TC].rearrange("(r p) t -> p r t", p=128),
                                   ob.rearrange("p (r t) -> p r t", t=TC)), 16)




def st_proj0(C):
    nc = C.nc
    xn_sb = load_packed(C, C.din("xn0"), D, T, name="xnsb0")
    st_proj_T(C, w="L0_attn_Wq", src_sb=xn_sb, out_name="q0", tag="p0q")
    st_proj_T(C, w="L0_attn_Wk", src_sb=xn_sb, out_name="k0", tag="p0k")
    st_proj_T(C, w="L0_attn_Wv", src_sb=xn_sb, out_name="v0", tag="p0v")
    proj_tok(C, "L0_attn_Wg", xn_sb, "gtok0", tag="p0g")
    alc = load_cols(C, C.win("L0_attn_A_log"), 16, name="alc")
    dtc = load_cols(C, C.win("L0_attn_dt_bias"), 16, name="dtc")
    bbc = load_cols(C, C.win("L0_attn_bb"), 16, name="bbc")
    coef = C.sbuf("coef", [16, 1], F32)
    nc.scalar.wait_ge(C.LW.h, C.LW.v)
    C.ACT.inc(nc.scalar.activation(coef[:, :], alc[:, :], AF.Exp), 1)
    nc.vector.wait_ge(C.ACT.h, C.ACT.v)
    vedge(C, nc.vector.tensor_scalar(coef[:, :], coef[:, :], -1.0, None, ALU.mult))
    gkd = C.dout("gk0")
    btd = C.dout("beta0")
    gbuf = C.sbuf("gkbuf", [16, T], F32)
    bbuf = C.sbuf("btbuf", [16, T], F32)
    tb1 = C.sbuf("gk_t1", [16, TC], F32)
    tb2 = C.sbuf("gk_t2", [16, TC], F32)
    tb3 = C.sbuf("gk_t3", [16, TC], F32)

    def cb_gk(mt, c, ps):
        sl = gbuf[:, c * TC:(c + 1) * TC]
        nc.scalar.wait_ge(C.PE.h, C.PE.v)
        nc.scalar.wait_ge(C.DVE.h, C.DVE.v)
        nc.scalar.activation(tb1[:, :], ps, AF.Abs, bias=dtc[:, 0:1])
        C.ACT.inc(nc.scalar.activation(tb1[:, :], tb1[:, :], AF.Exp, scale=-1.0), 1)
        nc.scalar.wait_ge(C.ACT.h, C.ACT.v)
        C.ACT.inc(nc.scalar.activation(tb2[:, :], tb1[:, :], AF.Ln, bias=C.cbias(1.0, p=16)), 1)
        nc.vector.wait_ge(C.ACT.h, C.ACT.v)
        vedge(C, nc.vector.tensor_scalar(tb3[:, :], ps, dtc[:, 0:1], 0.0, ALU.add, ALU.max))
        vedge(C, nc.vector.tensor_add(tb3[:, :], tb3[:, :], tb2[:, :]))
        i = nc.vector.tensor_scalar(sl, tb3[:, :], coef[:, 0:1], None, ALU.mult)
        return i

    gemm_T(C, terms=[("dram", C.win("L0_attn_Wgk"), xn_sb)], din=D, dout=16,
           out_cb=cb_gk, tag="g0k", nbanks=2, wait_first=True)
    nc.sync.wait_ge(C.DVE.h, C.DVE.v)
    C.ST.inc(nc.sync.dma_start(gkd[:, :], gbuf[:, :]), 16)

    def cb_bt(mt, c, ps):
        nc.scalar.wait_ge(C.PE.h, C.PE.v)
        i = nc.scalar.activation(bbuf[:, c * TC:(c + 1) * TC], ps, AF.Sigmoid, bias=bbc[:, 0:1])
        return i

    gemm_T(C, terms=[("dram", C.win("L0_attn_Wb"), xn_sb)], din=D, dout=16,
           out_cb=cb_bt, tag="g0b", nbanks=2, wait_first=True)
    nc.sync.wait_ge(C.DVE.h, C.DVE.v)
    C.ST.inc(nc.sync.dma_start(btd[:, :], bbuf[:, :]), 16)

def st_prep0(C):
    nc = C.nc
    hm = C.sbuf("p_hm", [128, 2], F32)
    bc = C.sbuf("p_bc", [2, 128], F32)
    C.LW.inc(nc.gpsimd.dma_start(hm[:, :], C.win("c_hmask2")[:, :]), 16)
    C.LW.inc(nc.gpsimd.dma_start(bc[:, :], C.win("c_bc2")[:, :]), 16)
    crope = C.sbuf("p_cr", [128, T], F32)
    srope = C.sbuf("p_sr", [128, T], F32)
    prope = C.sbuf("p_pr", [128, 128], BF16)
    C.LW.inc(nc.gpsimd.dma_start(crope[:, :], C.win("c_crope")[:, :]), 16)
    C.LW.inc(nc.gpsimd.dma_start(srope[:, :], C.win("c_srope")[:, :]), 16)
    C.LW.inc(nc.gpsimd.dma_start(prope[:, :], C.win("c_prope")[:, :]), 16)
    TP = T + 4
    xp = C.sbuf("p_xp", [128, RT * TP], BF16)
    cw = C.sbuf("p_cw", [128, RT * 4], F32)
    acc = C.sbuf("p_acc", [128, TC], F32)
    sil = C.sbuf("p_sil", [128, TC], F32)
    sq = C.sbuf("p_sq", [128, TC], F32)
    pn = C.psum("p_pn", [2, TC])
    nr = C.sbuf("p_nr", [2, TC], F32)
    rb = C.psum("p_rb", [128, TC])
    qn = C.sbuf("p_qn", [128, TC], BF16)
    psw = C.psum("p_psw", [128, TC], F32)
    t1 = C.sbuf("p_t1", [128, TC], F32)
    ob = C.sbuf("p_ob", [128, TC], BF16)
    for (src_n, dst_n, wname, dorope) in (("q0", "qr0", "L0_attn_conv_q_w", True),
                                          ("k0", "kr0", "L0_attn_conv_k_w", True),
                                          ("v0", "vs0", "L0_attn_conv_v_w", False)):
        srcd = C.din(src_n)
        dstd = C.dout(dst_n)
        nc.vector.wait_ge(C.ST.h, C.ST.v)
        C.DVE.inc(nc.vector.memset(xp.rearrange("p (r t) -> p r t", t=TP)[:, :, 0:4], 0.0), 1)
        nc.gpsimd.wait_ge(C.DVE.h, C.DVE.v)
        nc.gpsimd.wait_ge(C.PE.h, C.PE.v)
        C.LW.inc(nc.gpsimd.dma_start(xp.rearrange("p (r t) -> p r t", t=TP)[:, :, 4:4 + T],
                                     rview(srcd)), 16)
        C.LW.inc(nc.gpsimd.dma_start(cw.rearrange("p (r j) -> p r j", j=4),
                                     C.win(wname).rearrange("(r p) j -> p r j", p=128)), 16)
        nc.vector.wait_ge(C.LW.h, C.LW.v)
        nc.scalar.wait_ge(C.LW.h, C.LW.v)
        for r in range(RT):
            for c in range(CT):
                base = r * TP + 4 + c * TC
                vedge(C, nc.vector.tensor_scalar(acc[:, :], xp[:, base - 3: base - 3 + TC],
                                                 cw[:, r * 4:r * 4 + 1], None, ALU.mult))
                for j in range(1, 4):
                    vedge(C, nc.vector.scalar_tensor_tensor(
                        acc[:, :], xp[:, base - 3 + j: base - 3 + j + TC],
                        cw[:, r * 4 + j:r * 4 + j + 1], acc[:, :], ALU.mult, ALU.add))
                nc.scalar.wait_ge(C.DVE.h, C.DVE.v)
                C.ACT.inc(nc.scalar.activation(sil[:, :], acc[:, :], AF.Silu), 1)
                if not dorope:
                    nc.vector.wait_ge(C.ACT.h, C.ACT.v)
                    C.DVE.inc(nc.vector.tensor_copy(ob[:, :], sil[:, :]), 1)
                    nc.sync.wait_ge(C.DVE.h, C.DVE.v)
                    C.ST.inc(nc.sync.dma_start(dstd[r * 128:(r + 1) * 128, c * TC:(c + 1) * TC],
                                               ob[:, :]), 16)
                    nc.vector.wait_ge(C.ST.h, C.ST.v)
                    continue
                C.ACT.inc(nc.scalar.activation(sq[:, :], sil[:, :], AF.Square), 1)
                nc.tensor.wait_ge(C.ACT.h, C.ACT.v)
                nc.tensor.wait_ge(C.DVE.h, C.DVE.v)
                C.PE.inc(nc.tensor.matmul(pn[:, :], hm[:, :], sq[:, :], start=True, stop=True), 1)
                nc.vector.wait_ge(C.PE.h, C.PE.v)
                vedge(C, nc.vector.tensor_scalar(nr[:, :], pn[:, :], 1e-6, None, ALU.add))
                vedge(C, nc.vector.reciprocal(nr[:, :], nr[:, :]))
                nc.scalar.wait_ge(C.DVE.h, C.DVE.v)
                C.ACT.inc(nc.scalar.activation(nr[:, :], nr[:, :], AF.Sqrt), 1)
                nc.tensor.wait_ge(C.ACT.h, C.ACT.v)
                C.PE.inc(nc.tensor.matmul(rb[:, :], bc[:, :], nr[:, :], start=True, stop=True), 1)
                nc.vector.wait_ge(C.PE.h, C.PE.v)
                C.DVE.inc(nc.vector.tensor_mul(qn[:, :], sil[:, :], rb[:, :]), 1)
                nc.tensor.wait_ge(C.DVE.h, C.DVE.v)
                C.PE.inc(nc.tensor.matmul(psw[:, :], prope[:, :], qn[:, :], start=True, stop=True), 1)
                nc.vector.wait_ge(C.PE.h, C.PE.v)
                csl = slice(c * TC, (c + 1) * TC)
                vedge(C, nc.vector.tensor_mul(t1[:, :], qn[:, :], crope[:, csl]))
                vedge(C, nc.vector.tensor_mul(ob[:, :], psw[:, :], srope[:, csl]))
                C.DVE.inc(nc.vector.tensor_add(ob[:, :], ob[:, :], t1[:, :]), 1)
                nc.sync.wait_ge(C.DVE.h, C.DVE.v)
                C.ST.inc(nc.sync.dma_start(dstd[r * 128:(r + 1) * 128, c * TC:(c + 1) * TC],
                                           ob[:, :]), 16)
                nc.vector.wait_ge(C.ST.h, C.ST.v)


def st_scan0(C):
    nc = C.nc
    CH = 128
    NCH = T // CH
    gkd = C.din("gk0")
    btd = C.din("beta0")
    qd, kd, vd = C.din("qr0"), C.din("kr0"), C.din("vs0")
    cgd = C.dout("cgd0")
    cgld = C.dout("cgl0")
    gedd = C.dout("gedd0")
    outd = C.dout("otok0")
    ids = C.sbuf("s0id", [128, 128], F32)
    idb = C.sbuf("s0idb", [128, 128], BF16)
    trium = C.sbuf("s0trm", [128, 128], F32)
    triui = C.sbuf("s0tri", [128, 128], F32)
    C.LW.inc(nc.gpsimd.dma_start(ids[:, :], C.win("c_ident")[:, :]), 16)
    C.LW.inc(nc.gpsimd.dma_start(idb[:, :], C.win("c_ident")[:, :]), 16)
    C.LW.inc(nc.gpsimd.dma_start(trium[:, :], C.win("c_trium")[:, :]), 16)
    C.LW.inc(nc.gpsimd.dma_start(triui[:, :], C.win("c_triui")[:, :]), 16)
    ones16 = C.sbuf("s0o16", [16, CH], F32)
    onesr = C.sbuf("s0or", [1, 128], F32)
    gedbuf = C.sbuf("s0ged", [16, 32], F32)
    gedrow = C.sbuf("s0gedr", [1, 512], F32)
    gedB = C.sbuf("s0gedB", [64, 512], F32)
    colsA = C.sbuf("s0colA", [128, NCH * 128], F32)
    gcols = C.sbuf("s0gc", [128, NCH * 16], F32)
    ncols = C.sbuf("s0nc", [128, NCH * 16], F32)
    colsB = C.sbuf("s0colB", [128, NCH * 16], F32)
    gkr = C.sbuf("s0gk", [16, T], F32)
    rows0 = C.sbuf("s0r0", [128, T], F32)
    rows1 = C.sbuf("s0r1", [16, T], F32)
    tmp16 = C.sbuf("s0t16", [16, T], F32)
    C.LD.inc(nc.sync.dma_start(gkr[:, :], gkd[:, :]), 16)
    C.LD.inc(nc.sync.dma_start(rows0[32:48, :], btd[:, :]), 16)
    nc.vector.wait_ge(C.LD.h, C.LD.v)
    vedge(C, nc.vector.memset(ones16[:, :], 1.0))
    vedge(C, nc.vector.memset(onesr[:, :], 1.0))
    for n in range(NCH):
        cs = slice(n * CH, (n + 1) * CH)
        vedge(C, nc.vector.tensor_tensor_scan(rows0[0:16, cs], ones16[:, :], gkr[:, cs],
                                              0.0, ALU.mult, ALU.add))
        vedge(C, nc.vector.tensor_scalar(rows0[64:80, cs], rows0[0:16, cs],
                                         rows0[0:16, n * CH + 127:n * CH + 128], None, ALU.subtract))
        vedge(C, nc.vector.tensor_copy(gedbuf[:, n:n + 1], rows0[0:16, n * CH + 127:n * CH + 128]))
    nc.scalar.wait_ge(C.DVE.h, C.DVE.v)
    nc.scalar.activation(rows0[64:80, :], rows0[64:80, :], AF.Exp, scale=-1.0)
    C.ACT.inc(nc.scalar.activation(gedbuf[:, :], gedbuf[:, :], AF.Exp), 1)
    C.ACT.inc(nc.scalar.activation(tmp16[:, :], rows0[32:48, :], AF.Ln), 1)
    nc.vector.wait_ge(C.ACT.h, C.ACT.v)
    vedge(C, nc.vector.tensor_add(rows1[:, :], rows0[0:16, :], tmp16[:, :]))
    vedge(C, nc.vector.tensor_scalar(rows0[32:48, :], rows0[32:48, :], -1.0, None, ALU.mult))
    nc.sync.wait_ge(C.DVE.h, C.DVE.v)
    nc.sync.wait_ge(C.ACT.h, C.ACT.v)
    C.ST.inc(nc.sync.dma_start(cgd[:, :], rows0[0:16, :]), 16)
    C.ST.inc(nc.sync.dma_start(cgld[:, :], rows1[:, :]), 16)
    C.ST.inc(nc.sync.dma_start(gedd[:, :], gedbuf[:, :]), 16)
    nc.sync.wait_ge(C.ST.h, C.ST.v)
    C.LD.inc(nc.sync.dma_start(gedrow[0:1, :], gedd.rearrange("h n -> (h n)").rearrange("(one f) -> one f", one=1)), 16)
    psg = C.psum("s0psg", [128, 512])
    nc.tensor.wait_ge(C.LD.h, C.LD.v)
    nc.tensor.wait_ge(C.DVE.h, C.DVE.v)
    C.PE.inc(nc.tensor.matmul(psg[0:64, :], onesr[0:1, 0:64], gedrow[:, :], start=True, stop=True), 1)
    nc.vector.wait_ge(C.PE.h, C.PE.v)
    C.DVE.inc(nc.vector.tensor_copy(gedB[:, :], psg[0:64, :]), 1)
    psT = C.psum("s0psT", [128, 512])
    for n in range(NCH):
        cs = slice(n * CH, (n + 1) * CH)
        nc.tensor.wait_ge(C.DVE.h, C.DVE.v)
        nc.tensor.wait_ge(C.ACT.h, C.ACT.v)
        C.PE.inc(nc.tensor.transpose(psT[:, 0:128], rows0[:, cs], ids[:, :]), 1)
        nc.vector.wait_ge(C.PE.h, C.PE.v)
        C.DVE.inc(nc.vector.tensor_copy(colsA[:, n * 128:(n + 1) * 128], psT[:, 0:128]), 1)
        nc.tensor.wait_ge(C.DVE.h, C.DVE.v)
        C.PE.inc(nc.tensor.transpose(psT[:, 0:16], rows1[:, cs], ids[0:16, 0:16]), 1)
        nc.vector.wait_ge(C.PE.h, C.PE.v)
        C.DVE.inc(nc.vector.tensor_copy(colsB[:, n * 16:(n + 1) * 16], psT[:, 0:16]), 1)
        nc.scalar.wait_ge(C.DVE.h, C.DVE.v)
        C.ACT.inc(nc.scalar.activation(gcols[:, n * 16:(n + 1) * 16],
                                       colsA[:, n * 128:n * 128 + 16], AF.Exp), 1)
        nc.vector.wait_ge(C.ACT.h, C.ACT.v)
        C.DVE.inc(nc.vector.tensor_scalar(ncols[:, n * 16:(n + 1) * 16],
                                          colsA[:, n * 128:n * 128 + 16], -1.0, None, ALU.mult), 1)
    C.free(psg, psT)
    nc.sync.wait_ge(C.DVE.h, C.DVE.v)
    nc.sync.wait_ge(C.ACT.h, C.ACT.v)
    nc.sync.wait_ge(C.PE.h, C.PE.v)
    C.free(tmp16, rows1, rows0, gkr)
    psA = C.psum("s0psA", [128, 512])
    psB = C.psum("s0psB", [128, 512])
    psCc = C.psum("s0psC", [128, 512])
    psD = C.psum("s0psD", [128, 512])
    psE = C.psum("s0psE", [128, 512])
    psF = C.psum("s0psF", [128, 512])
    psG = C.psum("s0psG", [128, 1024], BF16)
    kh = [C.sbuf("s0kh", [64, T], BF16) for _ in range(2)]
    qh = [C.sbuf("s0qh", [64, T], BF16) for _ in range(2)]
    vh = [C.sbuf("s0vh", [64, T], BF16) for _ in range(2)]
    cgrow = [C.sbuf("s0cgr", [1, T], F32)]
    cglrow = [C.sbuf("s0cglr", [1, T], F32)]
    Ssb = [C.sbuf("s0S", [64, 64], F32) for _ in range(2)]
    Sbf = C.sbuf("s0Sbf", [64, 64], BF16)
    Ebt = C.sbuf("s0Ebt", [128, 128], F32)
    t1 = C.sbuf("s0t1", [128, 128], F32)
    Atb = C.sbuf("s0At", [128, 128], BF16)
    Ab = C.sbuf("s0A", [128, 128], BF16)
    pts = {k: C.sbuf(f"s0pt{k}", [128, 128], BF16) for k in (2, 4, 8, 16, 32, 64)}
    prs = {k: C.sbuf(f"s0pr{k}", [128, 128], BF16) for k in (2, 4, 8, 16, 32)}
    vts = C.sbuf("s0vts", [128, 64], F32)
    t2 = C.sbuf("s0t2", [128, 64], F32)
    Bb = C.sbuf("s0B", [128, 64], BF16)
    Xb = [C.sbuf("s0X", [128, 64], BF16) for _ in range(2)]
    Dl = C.sbuf("s0Dl", [128, 64], BF16)
    E2 = C.sbuf("s0E2", [128, 128], F32)
    t3 = C.sbuf("s0t3", [128, 128], F32)
    Mtb = C.sbuf("s0Mt", [128, 128], BF16)
    ktok = C.sbuf("s0ktok", [128, 64], BF16)
    o1sb = C.sbuf("s0o1", [128, 64], F32)
    otile = C.sbuf("s0ot", [128, 64], BF16)

    def load_head(h):
        p = h % 2
        C.LD.inc(nc.sync.dma_start(kh[p][:, :], kd[h * 64:(h + 1) * 64, :]), 16)
        C.LD.inc(nc.sync.dma_start(qh[p][:, :], qd[h * 64:(h + 1) * 64, :]), 16)
        C.LD.inc(nc.sync.dma_start(vh[p][:, :], vd[h * 64:(h + 1) * 64, :]), 16)

    nc.sync.wait_ge(C.DVE.h, C.DVE.v)
    load_head(0)
    pe_done = {}
    for h in range(16):
        p = h % 2
        if h >= 1:
            nc.sync.wait_ge(C.PE.h, pe_done[h - 1])
        C.LD.inc(nc.sync.dma_start(cgrow[0][0:1, :], cgd[h:h + 1, :]), 16)
        C.LD.inc(nc.sync.dma_start(cglrow[0][0:1, :], cgld[h:h + 1, :]), 16)
        if h + 1 < 16:
            load_head(h + 1)
        nc.vector.wait_ge(C.LD.h, C.LD.v)
        nc.tensor.wait_ge(C.LD.h, C.LD.v)
        vedge(C, nc.vector.memset(Ssb[0][:, :], 0.0))
        C.DVE.inc(nc.vector.memset(Sbf[:, :], 0.0), 1)
        nc.tensor.wait_ge(C.DVE.h, C.DVE.v)
        for n in range(NCH):
            cs = slice(n * CH, (n + 1) * CH)
            khc = kh[p][:, cs]
            qhc = qh[p][:, cs]
            vhc = vh[p][:, cs]
            ccol = lambda off: colsA[:, n * 128 + off + h: n * 128 + off + h + 1]
            gcol = gcols[:, n * 16 + h: n * 16 + h + 1]
            ncgcol = ncols[:, n * 16 + h: n * 16 + h + 1]
            nbcol = ccol(32)
            lcol = ccol(64)
            spar = n % 2
            nc.tensor.wait_ge(C.DVE.h, C.DVE.v)
            C.PE.inc(nc.tensor.matmul(psA[:, 0:128], khc, khc, start=True, stop=True), 1)
            C.PE.inc(nc.tensor.matmul(psB[:, 0:128], onesr[:, :], cglrow[0][0:1, cs],
                                      start=True, stop=True), 1)
            nc.scalar.wait_ge(C.PE.h, C.PE.v)
            C.ACT.inc(nc.scalar.activation(Ebt[:, :], psB[:, 0:128], AF.Exp, bias=ncgcol), 1)
            nc.vector.wait_ge(C.ACT.h, C.ACT.v)
            nc.vector.wait_ge(C.PE.h, C.PE.v)
            vedge(C, nc.vector.tensor_mul(t1[:, :], Ebt[:, :], psA[:, 0:128]))
            vedge(C, nc.vector.tensor_mul(Atb[:, :], t1[:, :], trium[:, :]))
            nc.tensor.wait_ge(C.DVE.h, C.DVE.v)
            C.PE.inc(nc.tensor.transpose(psG[:, 0:128], Atb[:, :], idb[:, :]), 1)
            nc.vector.wait_ge(C.PE.h, C.PE.v)
            vedge(C, nc.vector.tensor_copy(Ab[:, :], psG[:, 0:128]))
            prevt, prevr = Atb, Ab
            nc.tensor.wait_ge(C.DVE.h, C.DVE.v)
            for k in (2, 4, 8, 16, 32, 64):
                pp = psCc if (k % 4 == 2) else psD
                C.PE.inc(nc.tensor.matmul(pp[:, 0:128], prevr, prevt, start=True, stop=True), 1)
                nc.vector.wait_ge(C.PE.h, C.PE.v)
                vedge(C, nc.vector.tensor_copy(pts[k][:, :], pp[:, 0:128]))
                if k != 64:
                    nc.tensor.wait_ge(C.DVE.h, C.DVE.v)
                    C.PE.inc(nc.tensor.matmul(pp[:, 128:256], prevt, prevr, start=True, stop=True), 1)
                    nc.vector.wait_ge(C.PE.h, C.PE.v)
                    vedge(C, nc.vector.tensor_copy(prs[k][:, :], pp[:, 128:256]))
                    prevt, prevr = pts[k], prs[k]
                nc.tensor.wait_ge(C.DVE.h, C.DVE.v)
            C.PE.inc(nc.tensor.matmul(psE[:, 0:64], khc, Sbf[:, :], start=True, stop=True), 1)
            C.PE.inc(nc.tensor.transpose(psG[:, 128:192], vhc, idb[0:64, 0:64]), 1)
            nc.vector.wait_ge(C.PE.h, C.PE.v)
            vedge(C, nc.vector.tensor_copy(vts[:, :], psG[:, 128:192]))
            vedge(C, nc.vector.scalar_tensor_tensor(t2[:, :], psE[:, 0:64], gcol, vts[:, :],
                                                    ALU.mult, ALU.subtract))
            vedge(C, nc.vector.tensor_scalar(Bb[:, :], t2[:, :], nbcol, None, ALU.mult))
            X = Bb
            nc.tensor.wait_ge(C.DVE.h, C.DVE.v)
            xi = 0
            for k in (64, 32, 16, 8, 4, 2):
                C.PE.inc(nc.tensor.matmul(psF[:, 0:64], pts[k][:, :], X, start=True, stop=True), 1)
                nc.vector.wait_ge(C.PE.h, C.PE.v)
                vedge(C, nc.vector.tensor_add(Xb[xi][:, :], X, psF[:, 0:64]))
                X = Xb[xi]
                xi = 1 - xi
                nc.tensor.wait_ge(C.DVE.h, C.DVE.v)
            C.PE.inc(nc.tensor.matmul(psF[:, 0:64], Atb[:, :], X, start=True, stop=True), 1)
            nc.vector.wait_ge(C.PE.h, C.PE.v)
            vedge(C, nc.vector.tensor_sub(Xb[xi][:, :], X, psF[:, 0:64]))
            X = Xb[xi]
            vedge(C, nc.vector.tensor_scalar(Dl[:, :], X, lcol, None, ALU.mult))
            nc.tensor.wait_ge(C.DVE.h, C.DVE.v)
            C.PE.inc(nc.tensor.matmul(psA[:, 128:256], khc, qhc, start=True, stop=True), 1)
            C.PE.inc(nc.tensor.matmul(psB[:, 128:256], onesr[:, :], cgrow[0][0:1, cs],
                                      start=True, stop=True), 1)
            nc.scalar.wait_ge(C.PE.h, C.PE.v)
            C.ACT.inc(nc.scalar.activation(E2[:, :], psB[:, 128:256], AF.Exp, bias=ncgcol), 1)
            nc.vector.wait_ge(C.ACT.h, C.ACT.v)
            vedge(C, nc.vector.tensor_mul(t3[:, :], E2[:, :], psA[:, 128:256]))
            vedge(C, nc.vector.tensor_mul(Mtb[:, :], t3[:, :], triui[:, :]))
            nc.tensor.wait_ge(C.DVE.h, C.DVE.v)
            C.PE.inc(nc.tensor.matmul(psF[:, 64:128], Mtb[:, :], X, start=True, stop=True), 1)
            C.PE.inc(nc.tensor.matmul(psE[:, 64:128], qhc, Sbf[:, :], start=True, stop=True), 1)
            nc.vector.wait_ge(C.PE.h, C.PE.v)
            vedge(C, nc.vector.tensor_copy(o1sb[:, :], psF[:, 64:128]))
            vedge(C, nc.vector.scalar_tensor_tensor(otile[:, :], psE[:, 64:128], gcol,
                                                    o1sb[:, :], ALU.mult, ALU.add))
            nc.sync.wait_ge(C.DVE.h, C.DVE.v)
            C.ST.inc(nc.sync.dma_start(outd[n * CH:(n + 1) * CH, h * 64:(h + 1) * 64],
                                       otile[:, :]), 16)
            C.PE.inc(nc.tensor.transpose(psG[:, 192:256], khc, idb[0:64, 0:64]), 1)
            nc.vector.wait_ge(C.PE.h, C.PE.v)
            vedge(C, nc.vector.tensor_copy(ktok[:, :], psG[:, 192:256]))
            nc.tensor.wait_ge(C.DVE.h, C.DVE.v)
            C.PE.inc(nc.tensor.matmul(psE[0:64, 128:192], ktok[:, :], Dl[:, :], start=True, stop=True), 1)
            nc.vector.wait_ge(C.PE.h, C.PE.v)
            gedcol = gedB[:, h * 32 + n: h * 32 + n + 1]
            vedge(C, nc.vector.scalar_tensor_tensor(Ssb[1 - (n % 2)][0:64, :], Ssb[n % 2][0:64, :],
                                                    gedcol[0:64, :], psE[0:64, 128:192],
                                                    ALU.mult, ALU.add))
            C.DVE.inc(nc.vector.tensor_copy(Sbf[:, :], Ssb[1 - (n % 2)][0:64, :]), 1)
            nc.tensor.wait_ge(C.DVE.h, C.DVE.v)
        pe_done[h] = C.PE.v
        nc.vector.wait_ge(C.ST.h, C.ST.v)

def st_gate0(C):
    nc = C.nc
    od = C.din("otok0")
    gd = C.din("gtok0")
    outd = C.dout("ogtok0")
    gnw = C.sbuf("g0nw", [1, 64], F32)
    C.LW.inc(nc.gpsimd.dma_start(gnw[0:1, :], C.win("L0_attn_gnorm_w").rearrange("(one d) -> one d", one=1)), 16)
    onesc = C.sbuf("g0ones", [1, 128], F32)
    C.DVE.inc(nc.vector.memset(onesc[:, :], 1.0), 1)
    psb = C.psum("g0ps", [128, 512])
    gnb = C.sbuf("g0gnb", [128, 64], F32)
    nc.tensor.wait_ge(C.LW.h, C.LW.v)
    nc.tensor.wait_ge(C.DVE.h, C.DVE.v)
    C.PE.inc(nc.tensor.matmul(psb[:, 0:64], onesc[:, :], gnw[:, :], start=True, stop=True), 1)
    nc.vector.wait_ge(C.PE.h, C.PE.v)
    C.DVE.inc(nc.vector.tensor_copy(gnb[:, :], psb[:, 0:64]), 1)
    ot = [C.sbuf("g0ot", [128, D], BF16) for _ in range(2)]
    gt = [C.sbuf("g0gt", [128, D], BF16) for _ in range(2)]
    sq = C.sbuf("g0sq", [128, D], F32)
    red = C.sbuf("g0red", [128, 16], F32)
    sgl = C.sbuf("g0sg", [128, D], F32)
    ob = C.sbuf("g0ob", [128, D], BF16)

    def loadt(tt):
        pp = tt % 2
        C.LD.inc(nc.sync.dma_start(ot[pp][:, :], od[tt * 128:(tt + 1) * 128, :]), 16)
        C.LD.inc(nc.sync.dma_start(gt[pp][:, :], gd[tt * 128:(tt + 1) * 128, :]), 16)

    loadt(0)
    for tt in range(T // 128):
        pp = tt % 2
        if tt + 1 < T // 128:
            nc.sync.wait_ge(C.ST.h, C.ST.v)
            loadt(tt + 1)
        nc.scalar.wait_ge(C.LD.h, C.LD.v)
        nc.scalar.wait_ge(C.DVE.h, C.DVE.v)
        C.ACT.inc(nc.scalar.activation(sq[:, :], ot[pp][:, :], AF.Square), 1)
        C.ACT.inc(nc.scalar.activation(sgl[:, :], gt[pp][:, :], AF.Silu), 1)
        nc.vector.wait_ge(C.ACT.h, C.ACT.v)
        vedge(C, nc.vector.tensor_reduce(red[:, :], sq.rearrange("p (hh d) -> p hh d", d=64),
                                         AX.X, ALU.add))
        vedge(C, nc.vector.tensor_scalar(red[:, :], red[:, :], 1.0 / 64.0, None, ALU.mult))
        vedge(C, nc.vector.tensor_scalar(red[:, :], red[:, :], 1e-5, None, ALU.add))
        vedge(C, nc.vector.reciprocal(red[:, :], red[:, :]))
        nc.scalar.wait_ge(C.DVE.h, C.DVE.v)
        C.ACT.inc(nc.scalar.activation(red[:, :], red[:, :], AF.Sqrt), 1)
        nc.vector.wait_ge(C.ACT.h, C.ACT.v)
        r3 = red.rearrange("p (hh one) -> p hh one", one=1).broadcast_to([128, 16, 64])
        o3 = ot[pp].rearrange("p (hh d) -> p hh d", d=64)
        vedge(C, nc.vector.tensor_tensor(sq.rearrange("p (hh d) -> p hh d", d=64), o3, r3, ALU.mult))
        g3b = gnb.rearrange("p (one d) -> p one d", one=1).broadcast_to([128, 16, 64])
        vedge(C, nc.vector.tensor_tensor(sq.rearrange("p (hh d) -> p hh d", d=64),
                                         sq.rearrange("p (hh d) -> p hh d", d=64), g3b, ALU.mult))
        C.DVE.inc(nc.vector.tensor_mul(ob[:, :], sq[:, :], sgl[:, :]), 1)
        nc.sync.wait_ge(C.DVE.h, C.DVE.v)
        C.ST.inc(nc.sync.dma_start(outd[tt * 128:(tt + 1) * 128, :], ob[:, :]), 16)

def st_wo0(C):
    nc = C.nc
    og = load_packed(C, C.din("ogtok0"), T, D, name="w0og")
    ids = C.sbuf("w0id", [128, 128], BF16)
    C.LW.inc(nc.gpsimd.dma_start(ids[:, :], C.win("c_ident")[:, :]), 16)
    ogT = C.sbuf("w0ogT", [128, RT * T], BF16)
    psT = C.psum("w0ps", [128, 1024], BF16)
    nc.tensor.wait_ge(C.LW.h, C.LW.v)
    for tt in range(T // 128):
        for r in range(RT):
            nc.tensor.wait_ge(C.DVE.h, C.DVE.v)
            half = ((tt * RT + r) % 2) * 512
            C.PE.inc(nc.tensor.transpose(psT[:, half:half + 128],
                                         og[:, tt * D + r * 128: tt * D + (r + 1) * 128],
                                         ids[:, :]), 1)
            nc.vector.wait_ge(C.PE.h, C.PE.v)
            C.DVE.inc(nc.vector.tensor_copy(ogT[:, r * T + tt * 128: r * T + (tt + 1) * 128],
                                            psT[:, half:half + 128]), 1)
    C.free(psT)
    st_proj_T(C, w="L0_attn_Wo", src_sb=ogT, out_name="s0", tag="w0o")


def quant_tile(C, out_ap, in_ap, F, scr):
    nc = C.nc
    ALUc = ALU
    nb = F // 16
    amax, scale, rscale, absn, sgn, acc1, acc2, acc3, qq, xnorm = scr
    in3 = in_ap.rearrange("p (b s) -> p b s", s=16)
    vedge(C, nc.vector.tensor_reduce(amax[:, 0:nb], in3, AX.X, ALUc.max, apply_absolute_value=True))
    vedge(C, nc.vector.tensor_scalar(scale[:, 0:nb], amax[:, 0:nb], 1.0 / 6.0, None, ALUc.mult))
    vedge(C, nc.vector.tensor_scalar(scale[:, 0:nb], scale[:, 0:nb], 1e-12, None, ALUc.max))
    vedge(C, nc.vector.reciprocal(rscale[:, 0:nb], scale[:, 0:nb]))
    r3 = rscale[:, 0:nb].rearrange("p (b one) -> p b one", one=1).broadcast_to([128, nb, 16])
    vedge(C, nc.vector.tensor_tensor(xnorm[:, 0:F].rearrange("p (b s) -> p b s", s=16), in3, r3, ALUc.mult))
    nc.scalar.wait_ge(C.DVE.h, C.DVE.v)
    nc.scalar.activation(absn[:, 0:F], xnorm[:, 0:F], AF.Abs)
    C.ACT.inc(nc.scalar.activation(sgn[:, 0:F], xnorm[:, 0:F], AF.Sign), 1)
    nc.vector.wait_ge(C.ACT.h, C.ACT.v)
    vedge(C, nc.vector.tensor_scalar(acc1[:, 0:F], absn[:, 0:F], 0.25, None, ALUc.is_gt))
    for th in (0.75, 1.25, 1.75):
        vedge(C, nc.vector.scalar_tensor_tensor(acc1[:, 0:F], absn[:, 0:F], th, acc1[:, 0:F],
                                                ALUc.is_gt, ALUc.add))
    vedge(C, nc.vector.tensor_scalar(acc2[:, 0:F], absn[:, 0:F], 2.5, None, ALUc.is_gt))
    vedge(C, nc.vector.scalar_tensor_tensor(acc2[:, 0:F], absn[:, 0:F], 3.5, acc2[:, 0:F],
                                            ALUc.is_gt, ALUc.add))
    vedge(C, nc.vector.tensor_scalar(acc3[:, 0:F], absn[:, 0:F], 5.0, None, ALUc.is_gt))
    vedge(C, nc.vector.scalar_tensor_tensor(qq[:, 0:F], acc1[:, 0:F], 0.5, acc2[:, 0:F],
                                            ALUc.mult, ALUc.add))
    vedge(C, nc.vector.scalar_tensor_tensor(qq[:, 0:F], acc3[:, 0:F], 2.0, qq[:, 0:F],
                                            ALUc.mult, ALUc.add))
    vedge(C, nc.vector.tensor_mul(qq[:, 0:F], qq[:, 0:F], sgn[:, 0:F]))
    s3 = scale[:, 0:nb].rearrange("p (b one) -> p b one", one=1).broadcast_to([128, nb, 16])
    i = nc.vector.tensor_tensor(out_ap.rearrange("p (b s) -> p b s", s=16),
                                qq[:, 0:F].rearrange("p (b s) -> p b s", s=16), s3, ALUc.mult)
    vedge(C, i)

def mk_qscr(C, F, tag="qs"):
    return (C.sbuf(tag + "am", [128, F // 16], F32), C.sbuf(tag + "sc", [128, F // 16], F32),
            C.sbuf(tag + "rs", [128, F // 16], F32), C.sbuf(tag + "ab", [128, F], F32),
            C.sbuf(tag + "sg", [128, F], F32), C.sbuf(tag + "a1", [128, F], F32),
            C.sbuf(tag + "a2", [128, F], F32), C.sbuf(tag + "a3", [128, F], F32),
            C.sbuf(tag + "qq", [128, F], F32), C.sbuf(tag + "xn", [128, F], F32))

def st_ffn0a(C):
    nc = C.nc
    xnd = C.din("xn0")
    outd = C.dout("xqtok0")
    scr = mk_qscr(C, D, "qa")
    xt = [C.sbuf("qa_x", [128, D], BF16) for _ in range(2)]
    ob = C.sbuf("qa_o", [128, D], BF16)
    for tt in range(T // 128):
        pp = tt % 2
        nc.sync.wait_ge(C.DVE.h, C.DVE.v)
        C.LD.inc(nc.sync.dma_start(xt[pp][:, :], xnd[:, tt * 128:(tt + 1) * 128], transpose=True), 16)
        nc.vector.wait_ge(C.LD.h, C.LD.v)
        nc.vector.wait_ge(C.ST.h, C.ST.v)
        quant_tile(C, ob[:, :], xt[pp][:, :], D, scr)
        nc.sync.wait_ge(C.DVE.h, C.DVE.v)
        C.ST.inc(nc.sync.dma_start(outd[tt * 128:(tt + 1) * 128, :], ob[:, :]), 16)

def st_ffn0b(C):
    nc = C.nc
    xqd = C.din("xqtok0")
    h0d = C.dout("h0")
    wg = C.win("L0_ffn_w_gate")
    wu = C.win("L0_ffn_w_up")
    ids = C.sbuf("fbid", [128, 128], BF16)
    C.LW.inc(nc.gpsimd.dma_start(ids[:, :], C.win("c_ident")[:, :]), 16)
    xq = C.sbuf("fbxq", [128, RT * T], BF16)
    for r in range(RT):
        C.LD.inc(nc.sync.dma_start(xq[:, r * T:(r + 1) * T], xqd[:, r * 128:(r + 1) * 128],
                                   transpose=True), 16)
    scr = mk_qscr(C, D, "qb")
    wrow = C.sbuf("fbwr", [128, D], F32)
    wqt = C.sbuf("fbwqt", [128, D], BF16)
    wT = [C.sbuf("fbwT", [128, RT * 128], BF16) for _ in range(2)]
    psT = C.psum("fbpsT", [128, 1024], BF16)
    psA = C.psum("fbpsA", [128, TC])
    psB = C.psum("fbpsB", [128, TC])
    tsl = C.sbuf("fbts", [128, TC], F32)
    hsb = C.sbuf("fbh", [128, T], F32)
    NI = 2816 // 128
    for it in range(NI):
        for wi, wd in enumerate((wg, wu)):
            nc.gpsimd.wait_ge(C.PE.h, C.PE.v)
            nc.gpsimd.wait_ge(C.DVE.h, C.DVE.v)
            C.LW.inc(nc.gpsimd.dma_start(wrow.rearrange("p d -> p d"),
                                         wd[it * 128:(it + 1) * 128, :]), 16)
            nc.vector.wait_ge(C.LW.h, C.LW.v)
            quant_tile(C, wqt[:, :], wrow[:, :], D, scr)
            for k in range(RT):
                nc.tensor.wait_ge(C.DVE.h, C.DVE.v)
                C.PE.inc(nc.tensor.transpose(psT[:, (k % 8) * 128:(k % 8) * 128 + 128],
                                             wqt[:, k * 128:(k + 1) * 128], ids[:, :]), 1)
                nc.vector.wait_ge(C.PE.h, C.PE.v)
                C.DVE.inc(nc.vector.tensor_copy(wT[wi][:, k * 128:(k + 1) * 128],
                                                psT[:, (k % 8) * 128:(k % 8) * 128 + 128]), 1)
        nc.tensor.wait_ge(C.DVE.h, C.DVE.v)
        nc.tensor.wait_ge(C.LD.h, C.LD.v)
        for c in range(CT):
            for k in range(RT):
                nc.tensor.matmul(psA[:, :], wT[0][:, k * 128:(k + 1) * 128],
                                 xq[:, k * T + c * TC: k * T + (c + 1) * TC],
                                 start=(k == 0), stop=(k == RT - 1))
            for k in range(RT):
                mi = nc.tensor.matmul(psB[:, :], wT[1][:, k * 128:(k + 1) * 128],
                                      xq[:, k * T + c * TC: k * T + (c + 1) * TC],
                                      start=(k == 0), stop=(k == RT - 1))
            C.PE.inc(mi, 1)
            nc.scalar.wait_ge(C.PE.h, C.PE.v)
            C.ACT.inc(nc.scalar.activation(tsl[:, :], psA[:, :], AF.Silu), 1)
            nc.vector.wait_ge(C.ACT.h, C.ACT.v)
            i = nc.vector.tensor_mul(hsb[:, c * TC:(c + 1) * TC], tsl[:, :], psB[:, :])
            C.DVE.inc(i, 1)
            nc.tensor.wait_ge(C.DVE.h, C.DVE.v)
        nc.sync.wait_ge(C.DVE.h, C.DVE.v)
        C.ST.inc(nc.sync.dma_start(h0d[it * 128:(it + 1) * 128, :], hsb[:, :]), 16)
        nc.vector.wait_ge(C.ST.h, C.ST.v)

def st_ffn0c(C):
    nc = C.nc
    h0d = C.din("h0")
    outd = C.dout("hqtok0")
    NI = 2816 // 128
    ids = C.sbuf("fcid", [128, 128], F32)
    C.LW.inc(nc.gpsimd.dma_start(ids[:, :], C.win("c_ident")[:, :]), 16)
    hrow = C.sbuf("fch", [128, NI * 128], F32)
    htok = C.sbuf("fct", [128, 2816], F32)
    psT = C.psum("fcps", [128, 512])
    scr = mk_qscr(C, 2816, "qc")
    ob = C.sbuf("fco", [128, 2816], BF16)
    nc.tensor.wait_ge(C.LW.h, C.LW.v)
    for tt in range(T // 128):
        nc.gpsimd.wait_ge(C.DVE.h, C.DVE.v)
        C.LW.inc(nc.gpsimd.dma_start(hrow.rearrange("p (r t) -> p r t", t=128),
                                     h0d[:, tt * 128:(tt + 1) * 128].rearrange("(r p) t -> p r t", p=128)), 16)
        nc.tensor.wait_ge(C.LW.h, C.LW.v)
        for r in range(NI):
            nc.tensor.wait_ge(C.DVE.h, C.DVE.v)
            C.PE.inc(nc.tensor.transpose(psT[:, 0:128], hrow[:, r * 128:(r + 1) * 128], ids[:, :]), 1)
            nc.vector.wait_ge(C.PE.h, C.PE.v)
            C.DVE.inc(nc.vector.tensor_copy(htok[:, r * 128:(r + 1) * 128], psT[:, 0:128]), 1)
        nc.vector.wait_ge(C.ST.h, C.ST.v)
        quant_tile(C, ob[:, :], htok[:, :], 2816, scr)
        nc.sync.wait_ge(C.DVE.h, C.DVE.v)
        C.ST.inc(nc.sync.dma_start(outd[tt * 128:(tt + 1) * 128, :], ob[:, :]), 16)

def st_ffn0d(C):
    nc = C.nc
    hqd = C.din("hqtok0")
    wd = C.win("L0_ffn_w_down")
    outd = C.dout("g0")
    NI = 2816 // 128
    ids = C.sbuf("fdid", [128, 128], BF16)
    C.LW.inc(nc.gpsimd.dma_start(ids[:, :], C.win("c_ident")[:, :]), 16)
    scr = mk_qscr(C, 2816, "qd")
    wrow = C.sbuf("fdw", [128, 2816], F32)
    wqt = C.sbuf("fdwq", [128, 2816], BF16)
    psT = C.psum("fdpsT", [128, 1024], BF16)
    wT = C.sbuf("fdwT", [128, RT * NI * 128], BF16)
    nc.tensor.wait_ge(C.LW.h, C.LW.v)
    for dt in range(RT):
        nc.gpsimd.wait_ge(C.DVE.h, C.DVE.v)
        C.LW.inc(nc.gpsimd.dma_start(wrow[:, :], wd[dt * 128:(dt + 1) * 128, :]), 16)
        nc.vector.wait_ge(C.LW.h, C.LW.v)
        quant_tile(C, wqt[:, :], wrow[:, :], 2816, scr)
        for k in range(NI):
            nc.tensor.wait_ge(C.DVE.h, C.DVE.v)
            C.PE.inc(nc.tensor.transpose(psT[:, (k % 8) * 128:(k % 8) * 128 + 128],
                                         wqt[:, k * 128:(k + 1) * 128], ids[:, :]), 1)
            nc.vector.wait_ge(C.PE.h, C.PE.v)
            C.DVE.inc(nc.vector.tensor_copy(wT[:, (dt * NI + k) * 128:(dt * NI + k) * 128 + 128],
                                            psT[:, (k % 8) * 128:(k % 8) * 128 + 128]), 1)
    hq = C.sbuf("fdhq", [128, NI * TC], BF16)
    psO = [C.psum("fdpsO", [128, TC]) for _ in range(2)]
    ob = C.sbuf("fdob", [128, RT * TC], F32)
    for c in range(CT):
        nc.sync.wait_ge(C.PE.h, C.PE.v)
        for k in range(NI):
            C.LD.inc(nc.sync.dma_start(hq[:, k * TC:(k + 1) * TC],
                                       hqd[c * TC:(c + 1) * TC, k * 128:(k + 1) * 128],
                                       transpose=True), 16)
        nc.tensor.wait_ge(C.LD.h, C.LD.v)
        nc.tensor.wait_ge(C.DVE.h, C.DVE.v)
        for dt in range(RT):
            bank = dt % 2
            if dt >= 2:
                nc.tensor.wait_ge(C.ACT.h, C.ACT.v)
            for k in range(NI):
                mi = nc.tensor.matmul(psO[bank][:, :], wT[:, (dt * NI + k) * 128:(dt * NI + k) * 128 + 128],
                                      hq[:, k * TC:(k + 1) * TC],
                                      start=(k == 0), stop=(k == NI - 1))
            C.PE.inc(mi, 1)
            nc.scalar.wait_ge(C.PE.h, C.PE.v)
            C.ACT.inc(nc.scalar.activation(ob[:, dt * TC:(dt + 1) * TC], psO[bank][:, :], AF.Silu), 1)
        nc.sync.wait_ge(C.ACT.h, C.ACT.v)
        C.ST.inc(nc.sync.dma_start(outd[:, c * TC:(c + 1) * TC].rearrange("(r p) t -> p r t", p=128),
                                   ob.rearrange("p (r t) -> p r t", t=TC)), 16)
        nc.scalar.wait_ge(C.ST.h, C.ST.v)


STAGES = {
    "ln0": (lambda C: st_layernorm(C, 0), ["x0T"], ["xn0"]),
    "ln1": (lambda C: st_layernorm(C, 1), ["xres1"], ["xn1"]),
    "ln2": (lambda C: st_layernorm(C, 2), ["xres2"], ["xn2"]),
    "ln3": (lambda C: st_layernorm(C, 3), ["xres3"], ["xn3"]),
    "u0": (lambda C: st_update(C, 0), ["x0T", "s0", "g0"], ["xres1"]),
    "u1": (lambda C: st_update(C, 1), ["xres1", "s1", "g1"], ["xres2"]),
    "u2": (lambda C: st_update(C, 2), ["xres2", "s2", "g2"], ["xres3"]),
    "u3": (lambda C: st_update(C, 3), ["xres3", "s3", "g3"], ["xres4"]),
    "proj1": (st_proj1, ["xn1"], ["q1", "k1", "vT1"]),
    "l2n1": (st_l2norm1, ["q1", "k1"], ["qn1", "kn1"]),
    "attn1": (st_attn1, ["qn1", "kn1", "vT1"], ["of1"]),
    "wo1": (st_wo1, ["of1"], ["s1"]),
    "ffn1": (st_ffn1, ["xn1"], ["g1"]),
    "proj3": (st_proj3, ["xn3"], ["q3", "k3", "vT3"]),
    "attn3": (st_attn3, ["q3", "k3", "vT3"], ["of3", "m3"]),
    "wo3": (st_wo3, ["of3"], ["s3"]),
    "rms3": (st_rms3, ["xn3"], ["t3"]),
    "gate3": (lambda C: st_gatecomb(C, 3), ["t3"], ["comb3"]),
    "ffn3": (st_ffn3, ["t3", "comb3", "xn3"], ["g3"]),
    "spec2": (st_spec2, ["xn2"], ["u2"]),
    "scan2": (st_scan2, ["u2"], ["s2"]),
    "gate2": (lambda C: st_gatecomb(C, 2), ["xn2"], ["comb2"]),
    "ffn2": (st_ffn2, ["xn2", "comb2"], ["g2"]),
    "proj0": (st_proj0, ["xn0"], ["q0", "k0", "v0", "gtok0", "gk0", "beta0"]),
    "prep0": (st_prep0, ["q0", "k0", "v0"], ["qr0", "kr0", "vs0"]),
    "scan0": (st_scan0, ["qr0", "kr0", "vs0", "gk0", "beta0"], ["otok0"]),
    "gate0": (st_gate0, ["otok0", "gtok0"], ["ogtok0"]),
    "wo0": (st_wo0, ["ogtok0"], ["s0"]),
    "ffn0a": (st_ffn0a, ["xn0"], ["xqtok0"]),
    "ffn0b": (st_ffn0b, ["xqtok0"], ["h0"]),
    "ffn0c": (st_ffn0c, ["h0"], ["hqtok0"]),
    "ffn0d": (st_ffn0d, ["hqtok0"], ["g0"]),
}


def build_consts():
    c = {}
    p = np.arange(128)[:, None]
    cc = np.arange(1536)[None, :]
    c["c_mm"] = (cc >= p + 512).astype(np.float32)
    c["c_am"] = np.where((cc - 512) <= p, 0.0, NEG).astype(np.float32)
    c["c_hmask2"] = np.zeros((128, 2), np.float32)
    c["c_hmask2"][0:64, 0] = 1.0
    c["c_hmask2"][64:128, 1] = 1.0
    c["c_bc2"] = np.zeros((2, 128), np.float32)
    c["c_bc2"][0, 0:64] = 1.0
    c["c_bc2"][1, 64:128] = 1.0
    kk, dd = np.meshgrid(np.arange(D), np.arange(D), indexing="ij")
    ang = 2.0 * np.pi * ((kk * dd) % D).astype(np.float64) / D
    c["c_fr"] = np.cos(ang).astype(np.float32)
    c["c_fi"] = (-np.sin(ang)).astype(np.float32)
    c["c_fin"] = np.sin(ang).astype(np.float32)
    c["c_ident"] = np.eye(128, dtype=np.float32)
    c["c_am2"] = np.where(cc >= p + 512, 0.0, NEG).astype(np.float32)
    c["c_e2"] = np.zeros((16, 8), np.float32)
    for h in range(16):
        c["c_e2"][h, h // 2] = 1.0
    c["c_m16"] = np.zeros((16, 128), np.float32)
    for h in range(16):
        c["c_m16"][h, (h % 2) * 64:(h % 2) * 64 + 64] = 1.0
    c["c_iota32"] = np.broadcast_to(np.arange(32, dtype=np.float32), (128, 32)).copy()
    c["c_iotaB32"] = c["c_iota32"] + 1e6
    invf = (10000.0 ** (-np.arange(16) / 16.0))
    tt = np.arange(T)
    ang = np.outer(invf, tt)
    cr = np.ones((128, T), np.float32)
    sr = np.zeros((128, T), np.float32)
    for p in range(128):
        d = p % 64
        if d < 16:
            cr[p] = np.cos(ang[d]); sr[p] = -np.sin(ang[d])
        elif d < 32:
            cr[p] = np.cos(ang[d - 16]); sr[p] = np.sin(ang[d - 16])
    c["c_crope"] = cr
    c["c_srope"] = sr
    pr = np.zeros((128, 128), np.float32)
    for m in range(128):
        d = m % 64
        if d < 16:
            pr[m + 16, m] = 1.0
        elif d < 32:
            pr[m - 16, m] = 1.0
    c["c_prope"] = pr
    jj, ii2 = np.meshgrid(np.arange(128), np.arange(128), indexing="ij")
    c["c_trium"] = (ii2 > jj).astype(np.float32)
    c["c_triui"] = (ii2 >= jj).astype(np.float32)
    return c

HOST_SHAPES = {}

def host_arrays(inputs):
    consts = build_consts()
    shared = dict(consts)
    for k, v in inputs.items():
        if k != "L0_attn_x":
            shared[k] = np.ascontiguousarray(v)
    shared["w_nullkT"] = np.ascontiguousarray(inputs["L3_attn_null_k"][0, :, 0, :].T)
    shared["w_nullvr"] = np.ascontiguousarray(inputs["L3_attn_null_v"][0, :, 0, :].reshape(1, -1))
    shared["w_gate3T"] = np.ascontiguousarray(inputs["L3_ffn_gate_w"].T)
    perm = np.concatenate([np.arange(0, 2048, 2), np.arange(1, 2048, 2)])
    shared["w_l3w1T"] = np.ascontiguousarray(inputs["L3_ffn_w1"][:, perm, :].transpose(0, 2, 1))
    shared["w_l3b1p"] = np.ascontiguousarray(inputs["L3_ffn_b1"][:, perm])
    shared["w_l3w2T"] = np.ascontiguousarray(inputs["L3_ffn_w2"].transpose(0, 2, 1))
    shared["w_L2WinT"] = np.ascontiguousarray(inputs["L2_attn_W_in"].T)
    shared["w_L2WoutT"] = np.ascontiguousarray(inputs["L2_attn_W_out"].T)
    for k, v in shared.items():
        HOST_SHAPES[k] = tuple(v.shape)
    x = inputs["L0_attn_x"]
    per_core = []
    for b in range(x.shape[0]):
        d = dict(shared)
        d["x0T"] = np.ascontiguousarray(x[b].T)
        per_core.append(d)
    return per_core


def run_group(stage_names, per_core, core_ids, trace=False, export_all=True,
              needed_override=None):
    nc = bass.Bass("TRN2", target_bir_lowering=False)
    writes = set()
    for sn in stage_names:
        writes |= set(STAGES[sn][2])
    avail = set(per_core[core_ids[0]].keys())
    if needed_override is not None:
        needed = writes & needed_override
    else:
        needed = writes if export_all else (writes & {"xres4"})
    C = Ctx(nc, avail, needed, HOST_SHAPES)
    for sn in stage_names:
        STAGES[sn][0](C)
        C.end_stage()
    nc.sync.wait_ge(C.ST.h, C.ST.v) if C.ST.v else None
    in_maps = []
    for cid in core_ids:
        d = per_core[cid]
        in_maps.append({name: d[name] for name in C.ext_in})
    res = run_bass_kernel_spmd(nc, in_maps, core_ids=list(range(len(core_ids))), trace=trace)
    for ci, cid in enumerate(core_ids):
        for name in C.ext_out:
            per_core[cid][name] = res.results[ci][name]
    return res

GROUPS = [
    ["ln0", "proj0", "prep0", "scan0", "gate0", "wo0", "ffn0a", "ffn0b", "ffn0c", "ffn0d", "u0"],
    ["ln1", "proj1", "l2n1", "attn1", "wo1", "ffn1", "u1"],
    ["ln2", "spec2", "scan2", "gate2", "ffn2", "u2"],
    ["ln3", "proj3", "attn3", "wo3", "rms3", "gate3", "ffn3", "u3"],
]

def kernel(**inputs):
    per_core = {i: d for i, d in enumerate(host_arrays(inputs))}
    for gi, gps in enumerate(GROUPS):
        future_reads = {"xres4"}
        for g2 in GROUPS[gi + 1:]:
            for sn in g2:
                future_reads |= set(STAGES[sn][1])
        run_group(gps, per_core, list(range(NCORE)), needed_override=future_reads)
    out = np.stack([np.asarray(per_core[b]["xres4"], dtype=np.float32).T for b in range(NCORE)])
    return np.ascontiguousarray(out, dtype=np.float32)
